# version 11
# baseline (speedup 1.0000x reference)
import numpy as np
from contextlib import ExitStack

import concourse.bass as bass
import concourse.mybir as mybir
from concourse.bass_utils import run_bass_kernel_spmd

F32 = mybir.dt.float32
BF16 = mybir.dt.bfloat16
AF = mybir.ActivationFunctionType
ALU = mybir.AluOpType
AX = mybir.AxisListType

D = 1024
TOK = 2048
H = 16
SCALE = 0.125
E = 8
FF = 1408
NEG = -1.0e30

N_SUB = 4
DEBUG_TAPS = []
_CACHE = {}


class Sem:
    def __init__(self, h):
        self.h = h
        self.c = 0

    def bump(self, n=1):
        self.c += n
        return self.c


class Prog:
    def __init__(self):
        self.q = {e: [] for e in ("tensor", "vector", "scalar", "gpsimd", "sync")}
        self.bank_free = {}
        self.last_t = [None, False]

    def T(self, fn):
        def w(t, fn=fn):
            r = fn(t)
            if r is not None:
                self.last_t[0] = r
                self.last_t[1] = False
        self.q["tensor"].append(w)
    def V(self, fn): self.q["vector"].append(fn)
    def S(self, fn): self.q["scalar"].append(fn)
    def G(self, fn): self.q["gpsimd"].append(fn)
    def Y(self, fn): self.q["sync"].append(fn)

    def wait(self, eng, sem, val):
        if val <= 0:
            return
        self.q[eng].append(lambda e, s=sem.h, v=val: e.wait_ge(s, v))

    def t_mark(self, sem):
        val = sem.bump()
        def w(t, s=sem.h):
            inst, marked = self.last_t
            if inst is not None and not marked:
                inst.then_inc(s, 1)
                self.last_t[1] = True
            else:
                t.drain().then_inc(s, 1)
        self.q["tensor"].append(w)
        return val

    def v_mark(self, sem):
        val = sem.bump()
        self.V(lambda v, s=sem.h: v.drain().then_inc(s, 1))
        return val

    def s_mark(self, sem):
        val = sem.bump()
        self.S(lambda v, s=sem.h: v.drain().then_inc(s, 1))
        return val

    def VD(self):
        self.q["vector"].append(lambda v: v.drain())

    def SD(self):
        self.q["scalar"].append(lambda v: v.drain())

    def bank_acquire(self, bank):
        rel = self.bank_free.get(bank)
        if rel is not None:
            sem, val = rel
            self.wait("tensor", sem, val)
            self.bank_free[bank] = None

    def bank_release(self, bank, sem, val):
        self.bank_free[bank] = (sem, val)


def build_program(n_sub=N_SUB, taps=()):
    taps = list(taps)
    nc = bass.Bass("TRN2", target_bir_lowering=False)

    din = lambda name, shape, dt=F32: nc.dram_tensor(name, list(shape), dt, kind="ExternalInput")
    x0_h = din("x0", (TOK, D))
    aaWq_h = din("aaWq", (D, D))
    aaWkv_h = din("aaWkv", (D, 2 * D))
    aaBq_h = din("aaBq", (128, 8))
    aaBkv_h = din("aaBkv", (128, 16))
    abWqkv_h = din("abWqkv", (D, 3 * D))
    abB_h = din("abB", (128, 24))
    abBv_row_h = din("abBv_row", (1, D))
    abProjW_h = din("abProjW", (D, D))
    abProjB_h = din("abProjB", (128, 8))
    fRouter_h = din("fRouter", (D, E))
    fW1_h = din("fW1", (E, D, 2 * FF))
    fW2_h = din("fW2", (E, FF, D))
    eye_h = din("eye", (128, 128))
    eye2_h = din("eye2", (128, 64))
    Mi_h = din("Mi", (H, 128, 140))
    Me0_h = din("Me0", (H, 128, 140))
    Me15_h = din("Me15", (H, 128, 140))
    sel_h = din("sel", (1, 2))

    out_h = nc.dram_tensor("out", [TOK, D], F32, kind="ExternalOutput")
    w1bf_h = nc.dram_tensor("w1bf", [E, D, 2 * FF], BF16, kind="Internal")
    w2bf_h = nc.dram_tensor("w2bf", [E, FF, D], BF16, kind="Internal")
    probs_hbm = [nc.dram_tensor(f"probsh{i}", [E, TOK], F32, kind="Internal") for i in range(2)]
    cc_in = nc.dram_tensor("cc_in", [128, 96], F32, kind="Internal")
    cc_out = nc.dram_tensor("cc_out", [128, 96], F32, kind="Internal")
    dbg_h = {i: nc.dram_tensor(f"dbg{i}", [128, 8 * TOK], F32, kind="ExternalOutput")
             for i in taps}

    p = Prog()
    es = ExitStack()
    with es:
        sbuf = lambda name, shape, dt: es.enter_context(nc.sbuf_tensor(name, list(shape), dt))
        psum = lambda name, shape, dt: es.enter_context(nc.psum_tensor(name, list(shape), dt))
        mksem = lambda name: Sem(es.enter_context(nc.semaphore(name)))

        sT = mksem("sT")
        sV = mksem("sV")
        sS = mksem("sS")
        sG = mksem("sG")
        dW = mksem("dW")
        dY = mksem("dY")
        dO = mksem("dO")
        dCONST = mksem("dCONST")
        dX = mksem("dX")
        dSEL = mksem("dSEL")
        dWV = mksem("dWV")
        dQK = [mksem("dQK0"), mksem("dQK1")]
        dM = mksem("dM")
        dPJ = mksem("dPJ")
        dRW = mksem("dRW")
        dW1 = [mksem(f"dW1_{i}") for i in range(4)]
        dW2s = mksem("dW2")
        dPE = mksem("dPE")
        dPROBS = mksem("dPROBS")
        dWT1 = [mksem(f"dWT1_{i}") for i in range(4)]
        dWT2 = mksem("dWT2")
        dW1s2 = [mksem(f"dW1s2_{i}") for i in range(4)]
        dW2ss2 = mksem("dW2ss2")

        x_fm = sbuf("x_fm", (128, 8 * TOK), F32)
        xln = sbuf("xln", (128, 8 * TOK), BF16)
        BIGA = sbuf("BIGA", (128, 23712), BF16)
        WR1 = sbuf("WR1", (128, 11264), BF16)
        WSM = sbuf("WSM", (128, 8192), BF16)
        ones_b = sbuf("ones_b", (128, 128), BF16)
        eye_b = sbuf("eye_b", (128, 128), BF16)
        eye_f = sbuf("eye_f", (128, 128), F32)
        eye2_b = sbuf("eye2_b", (128, 64), BF16)
        aaBq = sbuf("aaBq_s", (128, 8), F32)
        aaBkv = sbuf("aaBkv_s", (128, 16), F32)
        abB = sbuf("abB_s", (128, 24), F32)
        abBv_row = sbuf("abBv_row_s", (1, D), BF16)
        abProjB = sbuf("abProjB_s", (128, 8), F32)
        selbuf = sbuf("selbuf", (128, 2), F32)
        epsb = sbuf("epsb", (1, 1), F32)
        ones_f = sbuf("ones_f", (128, 128), F32)
        sttbuf = sbuf("sttbuf", (128, 32), F32)

        pf = [psum(f"pf{i}", (128, 512), F32) for i in range(7)]

        def dma_g(out_ap, in_ap, sem=None):
            sem = sem or dW
            val = sem.bump(16)
            p.G(lambda e, o=out_ap, i=in_ap, s=sem.h: e.dma_start(o, i).then_inc(s, 16))
            return val

        def dma_y(out_ap, in_ap, sem=None):
            sem = sem or dY
            val = sem.bump(16)
            p.Y(lambda e, o=out_ap, i=in_ap, s=sem.h: e.dma_start(o, i).then_inc(s, 16))
            return val

        def dma_o(out_ap, in_ap, sem=None):
            sem = sem or dO
            val = sem.bump(16)
            p.G(lambda e, o=out_ap, i=in_ap, s=sem.h: e.dma_start(o, i).then_inc(s, 16))
            return val

        def mm(out_ap, pairs, extra=None):
            def f(t, out_ap=out_ap, pairs=list(pairs), extra=extra):
                n = len(pairs)
                last = None
                for i, (l, r) in enumerate(pairs):
                    last = t.matmul(out_ap, l, r, start=(i == 0),
                                    stop=(i == n - 1 and extra is None))
                if extra is not None:
                    last = t.matmul(out_ap, extra[0], extra[1], start=False, stop=True)
                return last
            p.T(f)

        dma_g(eye_f[:], eye_h.ap(), dCONST)
        dma_g(eye_b[:], eye_h.ap(), dCONST)
        dma_g(eye2_b[:], eye2_h.ap(), dCONST)
        dma_g(aaBq[:], aaBq_h.ap(), dCONST)
        dma_g(aaBkv[:], aaBkv_h.ap(), dCONST)
        dma_g(abB[:], abB_h.ap(), dCONST)
        dma_g(abBv_row[:], abBv_row_h.ap(), dCONST)
        dma_g(abProjB[:], abProjB_h.ap(), dCONST)
        consts_done = dCONST.c
        p.V(lambda v: v.memset(ones_b[:], 1.0))
        p.V(lambda v: v.memset(ones_f[:], 1.0))
        p.V(lambda v: v.memset(epsb[:], 1.0e-5))

        ones_done = p.v_mark(sV)
        dma_y(selbuf[:], bass.AP(sel_h, 0, [[0, 128], [1, 2]]), dSEL)
        sel_done = dSEL.c

        with nc.sbuf_tensor("xstage", [128, 4 * D], F32) as xstage:
            for rd in range(4):
                for i in range(4):
                    ti = rd * 4 + i
                    dma_y(xstage[:, i * D:(i + 1) * D],
                          x0_h.ap()[ti * 128:(ti + 1) * 128, :], dX)
                p.wait("tensor", dX, dX.c)
                p.wait("tensor", dCONST, consts_done)
                for kc in range(8):
                    bank = f"pf{kc % 4}"
                    p.bank_acquire(bank)
                    def ftr(t, kc=kc):
                        last = None
                        for i in range(4):
                            last = t.transpose(pf[kc % 4][:, i * 128:(i + 1) * 128],
                                               xstage[:, i * D + kc * 128:i * D + (kc + 1) * 128],
                                               eye_f[:])
                        return last
                    p.T(ftr)
                    tmv = p.t_mark(sT)
                    p.wait("vector", sT, tmv)
                    p.V(lambda v, rd=rd, kc=kc: v.tensor_copy(
                        x_fm[:, kc * TOK + rd * 512:kc * TOK + (rd + 1) * 512],
                        pf[kc % 4][:, 0:512]))
                    vmv = p.v_mark(sV)
                    p.bank_release(bank, sV, vmv)
                if rd < 3:
                    tmx = p.t_mark(sT)
                    p.wait("sync", sT, tmx)
        x_ready = p.v_mark(sV)

        _ln_n = [0]

        def emit_ln(dep, want_stt=False):
            bw0 = lambda: BIGA[:, 0:TOK]
            bw1 = lambda: BIGA[:, TOK:2 * TOK]
            _ln_n[0] += 1
            nn = _ln_n[0]
            with nc.sbuf_tensor(f"ln_row{nn}", [1, 3072], F32) as lnrow, \
                 nc.sbuf_tensor(f"ln_m{nn}", [1, TOK], BF16) as row_m, \
                 nc.sbuf_tensor(f"ln_r{nn}", [1, TOK], BF16) as row_r:
                for sem, val in dep:
                    p.wait("scalar", sem, val)
                    p.wait("vector", sem, val)
                    p.wait("tensor", sem, val)
                for pt in range(8):
                    p.S(lambda sc, pt=pt: sc.activation(
                        xln[:, pt * TOK:(pt + 1) * TOK],
                        x_fm[:, pt * TOK:(pt + 1) * TOK], AF.Copy))
                cast_done = p.s_mark(sS)
                p.wait("vector", sS, cast_done)
                p.wait("tensor", sS, cast_done)
                p.wait("tensor", sV, ones_done)
                for tc in range(4):
                    def fsq(v, tc=tc):
                        for pt in range(8):
                            sl = xln[:, pt * TOK + tc * 512:pt * TOK + (tc + 1) * 512]
                            v.tensor_mul(BIGA[:, 4096 + pt * 512:4096 + (pt + 1) * 512], sl, sl)
                    p.V(fsq)
                    sqv = p.v_mark(sV)
                    p.wait("tensor", sV, sqv)
                    p.bank_acquire("pf4")
                    p.bank_acquire("pf5")
                    def fmm(t, tc=tc):
                        for pt in range(8):
                            t.matmul(pf[4][0:1, 0:512], ones_b[:, 0:1],
                                     xln[:, pt * TOK + tc * 512:pt * TOK + (tc + 1) * 512],
                                     start=(pt == 0), stop=(pt == 7))
                        last = None
                        for pt in range(8):
                            last = t.matmul(pf[5][0:1, 0:512], ones_b[:, 0:1],
                                            BIGA[:, 4096 + pt * 512:4096 + (pt + 1) * 512],
                                            start=(pt == 0), stop=(pt == 7))
                        return last
                    p.T(fmm)
                    tmv = p.t_mark(sT)
                    p.wait("vector", sT, tmv)
                    c = tc * 512
                    p.V(lambda v, c=c: v.tensor_scalar_mul(lnrow[0:1, 0:512], pf[4][0:1, 0:512], 1.0 / 1024.0))
                    p.V(lambda v, c=c: v.tensor_scalar_mul(row_m[0:1, c:c + 512], pf[4][0:1, 0:512], -1.0 / 1024.0))
                    p.VD()
                    p.V(lambda v: v.tensor_mul(lnrow[0:1, 512:1024], lnrow[0:1, 0:512], lnrow[0:1, 0:512]))
                    p.VD()
                    p.V(lambda v: v.scalar_tensor_tensor(
                        lnrow[0:1, 1024:1536], pf[5][0:1, 0:512], 1.0 / 1024.0,
                        lnrow[0:1, 512:1024], op0=ALU.mult, op1=ALU.subtract))
                    if want_stt:
                        p.V(lambda v: v.tensor_scalar_mul(lnrow[0:1, 2048:2560],
                                                          lnrow[0:1, 0:512], -1.0))
                    stv = p.v_mark(sV)
                    p.bank_release("pf4", sV, stv)
                    p.bank_release("pf5", sV, stv)
                    p.wait("scalar", sV, stv)
                    p.S(lambda sc: sc.activation(lnrow[0:1, 1536:2048], lnrow[0:1, 1024:1536],
                                                 AF.Ln, bias=epsb[0:1, 0:1]))
                    p.SD()
                    p.S(lambda sc, c=c: sc.activation(row_r[0:1, c:c + 512], lnrow[0:1, 1536:2048],
                                                      AF.Exp, scale=-0.5))
                    if want_stt:
                        p.S(lambda sc: sc.activation(lnrow[0:1, 2560:3072], lnrow[0:1, 1536:2048],
                                                     AF.Exp, scale=-0.5))
                    lns = p.s_mark(sS)
                    p.wait("vector", sS, lns)
                    if want_stt:
                        if tc == 0:
                            p.bank_acquire("pf6")
                        p.wait("tensor", sS, lns)
                        p.wait("tensor", sV, stv)
                        def fst(t, tc=tc):
                            last = None
                            for j in range(4):
                                last = t.transpose(pf[6][:, tc * 4 + j:tc * 4 + j + 1],
                                                   lnrow[0:1, 2048 + j * 128:2048 + (j + 1) * 128],
                                                   eye_f[0:1, 0:1])
                                last = t.transpose(pf[6][:, 16 + tc * 4 + j:16 + tc * 4 + j + 1],
                                                   lnrow[0:1, 2560 + j * 128:2560 + (j + 1) * 128],
                                                   eye_f[0:1, 0:1])
                            return last
                        p.T(fst)
                        stt_t = p.t_mark(sT)
                        if tc == 3:
                            p.wait("vector", sT, stt_t)
                            p.V(lambda v: v.tensor_copy(sttbuf[:, 0:32], pf[6][:, 0:32]))
                            sttv = p.v_mark(sV)
                            p.bank_release("pf6", sV, sttv)
                rdone = p.s_mark(sS)
                p.wait("tensor", sS, rdone)
                for tc in range(4):
                    c = tc * 512
                    p.bank_acquire("pf4")
                    p.bank_acquire("pf5")
                    p.T(lambda t, c=c: t.matmul(pf[4][:, 0:512], ones_b[0:1, 0:128],
                                                row_m[0:1, c:c + 512], start=True, stop=True))
                    p.T(lambda t, c=c: t.matmul(pf[5][:, 0:512], ones_b[0:1, 0:128],
                                                row_r[0:1, c:c + 512], start=True, stop=True))
                    tmb = p.t_mark(sT)
                    p.wait("vector", sT, tmb)
                    p.V(lambda v, c=c: v.tensor_copy(bw0()[:, c:c + 512], pf[4][:, 0:512]))
                    p.V(lambda v, c=c: v.tensor_copy(bw1()[:, c:c + 512], pf[5][:, 0:512]))
                    bmv = p.v_mark(sV)
                    p.bank_release("pf4", sV, bmv)
                    p.bank_release("pf5", sV, bmv)
                def fno_a(v):
                    for pt in range(8):
                        sl = xln[:, pt * TOK:(pt + 1) * TOK]
                        v.tensor_add(sl, sl, bw0())
                p.V(fno_a)
                p.VD()
                def fno_b(v):
                    for pt in range(8):
                        sl = xln[:, pt * TOK:(pt + 1) * TOK]
                        v.tensor_mul(sl, sl, bw1())
                p.V(fno_b)
                return p.v_mark(sV)

        def emit_attn_a(xln_ready, wr1_free):
            va = BIGA[:, 0:16384]
            probsT = BIGA[:, 18432:20480]
            q2 = BIGA[:, 20480:22528]
            k2 = WSM[:, 0:2048]
            sv = WSM[:, 2048:2304]
            with nc.sbuf_tensor("aa_st", [128, 24], F32) as st, \
                 nc.sbuf_tensor("aa_pr", [128, 4096], BF16) as probs, \
                 nc.psum_tensor("pb16a", [128, 1024], BF16) as pb16:
                p.bank_acquire("b7")
                p.wait("vector", sV, ones_done)
                p.V(lambda v: v.memzero(pb16[:, 0:1024]))
                zv = p.v_mark(sV)
                p.bank_release("b7", sV, zv)
                p.wait("gpsimd", sT, wr1_free)
                wv_val = dma_g(
                    WR1[:, 0:8192].rearrange("p (kc c) -> p kc c", kc=8),
                    bass.AP(aaWkv_h, 1024, [[2 * D, 128], [128 * 2 * D, 8], [1, 1024]]),
                    dWV)
                p.wait("tensor", dWV, wv_val)
                p.wait("tensor", sV, xln_ready)
                for vc in range(8):
                    for tc in range(4):
                        ib = (vc * 4 + tc) % 4
                        bank = f"pf{ib}"
                        p.bank_acquire(bank)
                        mm(pf[ib][:, 0:512],
                           [(WR1[:, kc * 1024 + vc * 128:kc * 1024 + (vc + 1) * 128],
                             xln[:, kc * TOK + tc * 512:kc * TOK + (tc + 1) * 512])
                            for kc in range(8)])
                        tmv = p.t_mark(sT)
                        p.wait("vector", sT, tmv)
                        p.V(lambda v, vc=vc, tc=tc, ib=ib: v.tensor_scalar_add(
                            va[:, vc * TOK + tc * 512:vc * TOK + (tc + 1) * 512],
                            pf[ib][:, 0:512], aaBkv[:, 8 + vc:9 + vc]))
                        vmv = p.v_mark(sV)
                        p.bank_release(bank, sV, vmv)
                wv_used = sT.c

                qk_rel = [wv_used, wv_used]
                last_res = 0
                tr_mark = {}
                st_res = [0, 0]

                def aa_p1a(h):
                    pb = (h % 2) * 64
                    r = h % 4
                    sc_marks = []
                    for qc in range(4):
                        p.bank_acquire(f"pf{qc}")
                        p.T(lambda t, qc=qc, pb=pb, r=r: t.matmul(
                            pf[qc][:, 0:512],
                            q2[pb:pb + 64, 512 * qc + r:512 * qc + r + 509:4],
                            k2[pb:pb + 64, r:r + 2045:4],
                            start=True, stop=True))
                        sc_marks.append(p.t_mark(sT))
                    return sc_marks

                def aa_p1b(h, sc_marks):
                    sl = h % 2
                    for qc in range(4):
                        p.wait("vector", sT, sc_marks[qc])
                        p.V(lambda v, qc=qc, sl=sl: v.reduce_max(st[:, sl * 12 + qc:sl * 12 + qc + 1],
                                                                 pf[qc][:, 0:512], axis=AX.X))
                    p.VD()
                    for qc in range(4):
                        p.V(lambda v, qc=qc, sl=sl: v.tensor_scalar_mul(
                            st[:, sl * 12 + 4 + qc:sl * 12 + 5 + qc],
                            st[:, sl * 12 + qc:sl * 12 + qc + 1], -SCALE))
                    mx_done = p.v_mark(sV)
                    p.wait("scalar", sV, mx_done)
                    if h >= 2:
                        p.wait("scalar", sT, tr_mark[h - 2])
                    for qc in range(4):
                        p.S(lambda sc, qc=qc, sl=sl: sc.activation(
                            probs[:, sl * 2048 + qc * 512:sl * 2048 + (qc + 1) * 512], pf[qc][:, 0:512],
                            AF.Exp, scale=SCALE, bias=st[:, sl * 12 + 4 + qc:sl * 12 + 5 + qc],
                            accum_out=st[:, sl * 12 + 8 + qc:sl * 12 + 9 + qc]))
                    exp_done = p.s_mark(sS)
                    for qc in range(4):
                        p.bank_release(f"pf{qc}", sS, exp_done)
                    p.wait("vector", sS, exp_done)
                    p.wait("vector", sV, st_res[sl])
                    p.V(lambda v, sl=sl: v.reciprocal(st[:, sl * 12 + 8:sl * 12 + 12],
                                                      st[:, sl * 12 + 8:sl * 12 + 12]))
                    p.VD()
                    def fnorm(v, sl=sl):
                        for qc in range(4):
                            v.tensor_scalar_mul(probs[:, sl * 2048 + qc * 512:sl * 2048 + (qc + 1) * 512],
                                                probs[:, sl * 2048 + qc * 512:sl * 2048 + (qc + 1) * 512],
                                                st[:, sl * 12 + 8 + qc:sl * 12 + 9 + qc])
                    p.V(fnorm)
                    return p.v_mark(sV)

                def aa_p2(h, probs_done):
                    nonlocal last_res
                    sl = h % 2
                    pb = (h % 2) * 64
                    r = h % 4
                    vblk = h // 2
                    p.wait("tensor", sV, probs_done)
                    for kcp in range(2):
                        p.bank_acquire("b7")
                        def ftr1(t, kcp=kcp, sl=sl):
                            last = None
                            for k2_ in range(2):
                                kc = kcp * 2 + k2_
                                for qc in range(4):
                                    last = t.transpose(
                                        pb16[:, k2_ * 512 + qc * 128:k2_ * 512 + (qc + 1) * 128],
                                        probs[:, sl * 2048 + qc * 512 + kc * 128:sl * 2048 + qc * 512 + (kc + 1) * 128],
                                        eye_b[:])
                            return last
                        p.T(ftr1)
                        tmv = p.t_mark(sT)
                        p.wait("vector", sT, tmv)
                        p.V(lambda v, kcp=kcp: v.tensor_copy(
                            probsT[:, kcp * 1024:(kcp + 1) * 1024], pb16[:, 0:1024]))
                        evv = p.v_mark(sV)
                        p.bank_release("b7", sV, evv)
                    p.bank_acquire("b7")
                    def fsv(t, pb=pb, r=r, vblk=vblk):
                        last = None
                        for kc in range(4):
                            last = t.transpose(pb16[:, 512 + kc * 64:512 + (kc + 1) * 64],
                                               va[pb:pb + 64, vblk * TOK + 512 * kc + r:vblk * TOK + 512 * kc + r + 509:4],
                                               eye2_b[pb:pb + 64, 0:64])
                        return last
                    p.T(fsv)
                    tr_mark[h] = p.t_mark(sT)
                    p.wait("vector", sT, tr_mark[h])
                    p.V(lambda v: v.tensor_copy(sv[:, 0:256], pb16[:, 512:768]))
                    svv = p.v_mark(sV)
                    p.bank_release("b7", sV, svv)
                    p.wait("tensor", sV, svv)
                    p.bank_acquire("pf6")
                    mm(pf[6][pb:pb + 64, 0:512],
                       [(sv[:, kc * 64:(kc + 1) * 64], probsT[:, kc * 512:(kc + 1) * 512])
                        for kc in range(4)])
                    ctxv = p.t_mark(sT)
                    p.wait("vector", sT, ctxv)
                    p.V(lambda v, pb=pb, r=r, vblk=vblk: v.tensor_add(
                        x_fm[pb:pb + 64, vblk * TOK + r:vblk * TOK + r + 2045:4],
                        x_fm[pb:pb + 64, vblk * TOK + r:vblk * TOK + r + 2045:4],
                        pf[6][pb:pb + 64, 0:512]))
                    last_res = p.v_mark(sV)
                    st_res[sl] = last_res
                    p.bank_release("pf6", sV, last_res)

                pend = []
                sc_pend = None
                for pr in range(8):
                    slot = pr % 2
                    p.wait("gpsimd", sT, qk_rel[slot])
                    base = slot * 2048
                    dma_g(WR1[:, base:base + 1024].rearrange("p (kc c) -> p kc c", kc=8),
                          bass.AP(aaWq_h, pr * 128, [[D, 128], [128 * D, 8], [1, 128]]),
                          dQK[slot])
                    qk_val = dma_g(WR1[:, base + 1024:base + 2048].rearrange("p (kc c) -> p kc c", kc=8),
                                   bass.AP(aaWkv_h, pr * 128, [[2 * D, 128], [128 * 2 * D, 8], [1, 128]]),
                                   dQK[slot])
                    p.wait("tensor", dQK[slot], qk_val)
                    for which, bias_ap in ((0, aaBq[:, pr:pr + 1]), (1, aaBkv[:, pr:pr + 1])):
                        dst = q2 if which == 0 else k2
                        for tc in range(4):
                            ib = 4 + tc % 2
                            p.bank_acquire(f"pf{ib}")
                            mm(pf[ib][:, 0:512],
                               [(WR1[:, base + which * 1024 + kc * 128:base + which * 1024 + (kc + 1) * 128],
                                 xln[:, kc * TOK + tc * 512:kc * TOK + (tc + 1) * 512])
                                for kc in range(8)])
                            tmv = p.t_mark(sT)
                            p.wait("vector", sT, tmv)
                            p.V(lambda v, dst=dst, b=bias_ap, tc=tc, ib=ib: v.tensor_scalar_add(
                                dst[:, tc * 512:(tc + 1) * 512], pf[ib][:, 0:512], b))
                            vmv = p.v_mark(sV)
                            p.bank_release(f"pf{ib}", sV, vmv)
                    qk_rel[slot] = sT.c
                    qkv_ready = p.v_mark(sV)
                    p.wait("tensor", sV, qkv_ready)
                    for hh in range(2):
                        h = 2 * pr + hh
                        scm = aa_p1a(h)
                        if pend:
                            aa_p2(*pend.pop(0))
                        pd = aa_p1b(h, scm)
                        pend.append((h, pd))
                while pend:
                    aa_p2(*pend.pop(0))
                return last_res, sT.c

        def emit_ffn(ci, xln_ready, wr1_free, wsm_free, prev_done):
            g = BIGA
            with nc.psum_tensor(f"pf7_{ci}", [128, 512], F32) as pf7, \
                 nc.sbuf_tensor(f"f_peb{ci}", [128, TOK], F32) as peb, \
                 nc.sbuf_tensor(f"f_st{ci}", [128, 1536], BF16) as stmp, \
                 nc.sbuf_tensor(f"f_y2{ci}", [128, 1536], BF16) as y2tmp:
                with nc.sbuf_tensor(f"f_rw{ci}", [128, 64], F32) as rwf, \
                     nc.sbuf_tensor(f"f_rs{ci}", [128, 32], F32) as rs, \
                     nc.sbuf_tensor(f"f_sr{ci}", [1, 8], F32) as srow, \
                     nc.sbuf_tensor(f"f_sb{ci}", [128, 8], F32) as Sb, \
                     nc.sbuf_tensor(f"f_pr{ci}", [128, 64], F32) as prbB, \
                     nc.sbuf_tensor(f"f_t8{ci}", [128, 64], F32) as t64:
                    p.wait("gpsimd", sV, xln_ready)
                    rw_val = dma_g(rwf[:].rearrange("p (kc c) -> p kc c", kc=8),
                                   bass.AP(fRouter_h, 0, [[E, 128], [128 * E, 8], [1, E]]),
                                   dRW)
                    p.wait("tensor", dRW, rw_val)
                    p.wait("tensor", sV, xln_ready)
                    p.bank_acquire("pf5")
                    mm(pf[5][0:1, 0:8],
                       [(ones_f[:, 0:1], rwf[:, kc * 8:(kc + 1) * 8]) for kc in range(8)])
                    tms = p.t_mark(sT)
                    p.wait("vector", sT, tms)
                    p.V(lambda v: v.tensor_copy(srow[0:1, 0:8], pf[5][0:1, 0:8]))
                    smv = p.v_mark(sV)
                    p.bank_release("pf5", sV, smv)
                    p.wait("tensor", sV, smv)
                    p.bank_acquire("pf5")
                    p.T(lambda t: t.matmul(pf[5][:, 0:8], ones_f[0:1, 0:128],
                                           srow[0:1, 0:8], start=True, stop=True))
                    tms2 = p.t_mark(sT)
                    p.wait("vector", sT, tms2)
                    p.V(lambda v: v.tensor_copy(Sb[:], pf[5][:, 0:8]))
                    sbv = p.v_mark(sV)
                    p.bank_release("pf5", sV, sbv)
                    def pf6_v():
                        return bass.AP(pf[6], 0, [[512, 128], [8, 8], [0 + 1 - 1, 8]]) if False else bass.AP(pf[6], 0, [[512, 128], [8, 8], [1, 8]])
                    def pv3(buf):
                        return bass.AP(buf, 0, [[64, 128], [8, 8], [1, 8]])
                    with nc.sbuf_tensor(f"f_stag{ci}", [64, 128], F32) as stag:
                      for bt in range(2):
                        p.bank_acquire("pf6")
                        for tkl in range(8):
                            tk = bt * 8 + tkl
                            mm(pf[6][0:128, tkl * 8:(tkl + 1) * 8],
                               [(x_fm[:, kc * TOK + tk * 128:kc * TOK + (tk + 1) * 128],
                                 rwf[:, kc * 8:(kc + 1) * 8]) for kc in range(8)])
                        tmv = p.t_mark(sT)
                        p.wait("vector", sT, tmv)
                        st_m = sttbuf[:, bt * 8:(bt + 1) * 8]
                        st_r = sttbuf[:, 16 + bt * 8:16 + (bt + 1) * 8]
                        rbc = bass.AP(sttbuf, 16 + bt * 8, [[32, 128], [1, 8], [0, 8]])
                        mrbc = bass.AP(rs, 0, [[32, 128], [1, 8], [0, 8]])
                        m1bc = bass.AP(rs, 8, [[32, 128], [1, 8], [0, 8]])
                        sumbc = bass.AP(rs, 16, [[32, 128], [1, 8], [0, 8]])
                        m2bc = bass.AP(rs, 24, [[32, 128], [1, 8], [0, 8]])
                        sbbc = bass.AP(Sb, 0, [[8, 128], [0, 8], [1, 8]])
                        p3 = lambda ap: bass.AP(ap.tensor, ap.offset, [[64, 128], [8, 8], [1, 8]])
                        p.V(lambda v, st_m=st_m, st_r=st_r: v.tensor_mul(rs[:, 0:8], st_m, st_r))
                        p.V(lambda v, rbc=rbc: v.tensor_tensor(prbB[:, 0:64], pf6_v(), rbc, op=ALU.mult))
                        p.VD()
                        p.V(lambda v, sbbc=sbbc, mrbc=mrbc: v.tensor_tensor(t64[:, 0:64], sbbc, mrbc, op=ALU.mult))
                        p.VD()
                        p.V(lambda v: v.tensor_add(prbB[:, 0:64], prbB[:, 0:64], t64[:, 0:64]))
                        m1 = p.v_mark(sV)
                        p.bank_release("pf6", sV, m1)
                        p.V(lambda v: v.tensor_reduce(rs[:, 8:16], pv3(prbB), axis=AX.X, op=ALU.max))
                        p.VD()
                        p.V(lambda v, m1bc=m1bc: v.tensor_tensor(prbB[:, 0:64], prbB[:, 0:64], m1bc, op=ALU.subtract))
                        sm1 = p.v_mark(sV)
                        p.wait("scalar", sV, sm1)
                        p.S(lambda sc: sc.activation(prbB[:, 0:64], prbB[:, 0:64], AF.Exp))
                        e1 = p.s_mark(sS)
                        p.wait("vector", sS, e1)
                        p.V(lambda v: v.tensor_reduce(rs[:, 16:24], pv3(prbB), axis=AX.X, op=ALU.add))
                        p.VD()
                        p.V(lambda v: v.reciprocal(rs[:, 16:24], rs[:, 16:24]))
                        p.VD()
                        p.V(lambda v, sumbc=sumbc: v.tensor_tensor(prbB[:, 0:64], prbB[:, 0:64], sumbc, op=ALU.mult))
                        p.VD()
                        p.V(lambda v: v.tensor_reduce(rs[:, 8:16], pv3(prbB), axis=AX.X, op=ALU.max))
                        p.VD()
                        p.V(lambda v, m1bc=m1bc: v.tensor_tensor(t64[:, 0:64], prbB[:, 0:64], m1bc, op=ALU.is_equal))
                        p.VD()
                        p.V(lambda v: v.tensor_scalar_mul(t64[:, 0:64], t64[:, 0:64], 1.0e30))
                        p.VD()
                        p.V(lambda v: v.tensor_sub(t64[:, 0:64], prbB[:, 0:64], t64[:, 0:64]))
                        p.VD()
                        p.V(lambda v: v.tensor_reduce(rs[:, 24:32], pv3(t64), axis=AX.X, op=ALU.max))
                        p.VD()
                        p.V(lambda v, m2bc=m2bc: v.tensor_tensor(t64[:, 0:64], prbB[:, 0:64], m2bc, op=ALU.is_ge))
                        p.VD()
                        p.V(lambda v: v.tensor_mul(prbB[:, 0:64], prbB[:, 0:64], t64[:, 0:64]))
                        lpv = p.v_mark(sV)
                        p.wait("tensor", sV, lpv)
                        p.bank_acquire("pf5")
                        p.T(lambda t: t.transpose(pf[5][0:64, 0:128], prbB[:, 0:64], eye_f[:]))
                        ttv = p.t_mark(sT)
                        p.wait("vector", sT, ttv)
                        p.wait("vector", dPROBS, dPROBS.c)
                        p.V(lambda v: v.tensor_copy(stag[:, 0:128], pf[5][0:64, 0:128]))
                        pfv = p.v_mark(sV)
                        p.bank_release("pf5", sV, pfv)
                        p.wait("sync", sV, pfv)
                        probs_w = dma_y(
                            bass.AP(probs_hbm[ci], bt * 8192, [[128, 64], [1, 128]]),
                            stag[:, 0:128], dPROBS)
                w2_src = w2bf_h if ci == 1 else fW2_h
                w1_src = w1bf_h if ci == 1 else fW1_h
                w_eng = "sync" if ci == 1 else "gpsimd"
                W1S = dW1s2 if ci == 1 else dW1
                W2S = dW2ss2 if ci == 1 else dW2s

                def dma_w(dst, src_ap, sem):
                    return dma_y(dst, src_ap, sem) if ci == 1 else dma_g(dst, src_ap, sem)

                w2_vals = {}
                def load_w2(e, guard):
                    p.wait(w_eng, sT, guard)
                    p.wait(w_eng, dWT2, dWT2.c)
                    v = dma_w(WR1[:, 0:11264].rearrange("p (fc c) -> p fc c", fc=11),
                              bass.AP(w2_src, e * FF * D, [[D, 128], [128 * D, 11], [1, D]]),
                              W2S)
                    w2_vals[e] = v
                    if ci == 0:
                        p.wait("gpsimd", W2S, v)
                        dma_o(bass.AP(w2bf_h, e * FF * D, [[D, 128], [128 * D, 11], [1, D]]),
                              WR1[:, 0:11264].rearrange("p (fc c) -> p fc c", fc=11),
                              dWT2)
                    return v

                w1_vals = {}
                def load_w1(e, fc, slot, guard):
                    p.wait(w_eng, sT, guard)
                    p.wait(w_eng, dWT1[slot], dWT1[slot].c)
                    base = e * D * 2 * FF + fc * 128
                    for half in range(2):
                        dst = bass.AP(WSM, slot * 2048 + half * 128,
                                      [[8192, 128], [256, 8], [1, 128]])
                        src = bass.AP(w1_src, base + half * FF,
                                      [[2 * FF, 128], [128 * 2 * FF, 8], [1, 128]])
                        v = dma_w(dst, src, W1S[slot])
                    w1_vals[(e, fc)] = (slot, v)
                    if ci == 0:
                        p.wait("gpsimd", W1S[slot], v)
                        for half in range(2):
                            dma_o(bass.AP(w1bf_h, base + half * FF,
                                          [[2 * FF, 128], [128 * 2 * FF, 8], [1, 128]]),
                                  bass.AP(WSM, slot * 2048 + half * 128,
                                          [[8192, 128], [256, 8], [1, 128]]),
                                  dWT1[slot])
                    return v

                chunks = [(e, fc) for e in range(E) for fc in range(11)]
                slot_of = {cch: i % 4 for i, cch in enumerate(chunks)}
                load_w2(0, wr1_free)
                for ii in range(4):
                    load_w1(*chunks[ii], slot_of[chunks[ii]], wsm_free)
                pe_vals = {}
                p.wait("sync", dPROBS, probs_w)
                for bt_ in range(2):
                    pe_vals[0] = dma_y(peb[:, bt_ * 1024:(bt_ + 1) * 1024],
                                       bass.AP(probs_hbm[ci], bt_ * 8192 + 0 * 128,
                                               [[0, 128], [1024, 8], [1, 128]]), dPE)

                o_done = prev_done
                tmp_guard = [0, 0, 0]
                for e in range(E):
                    p.wait("vector", dPE, pe_vals[e])
                    p.wait("vector", sT, o_done)
                    def ybank(tc):
                        return pf[tc]
                    def y2bank(tc):
                        return pf[4 + tc] if tc < 3 else pf7
                    def y2key(tc):
                        return f"pf{4 + tc}" if tc < 3 else "b7"
                    for fc in range(11):
                        slot = slot_of[(e, fc)]
                        w1s, w1v = w1_vals[(e, fc)]
                        p.wait("tensor", W1S[w1s], w1v)
                        for tc in range(4):
                            p.bank_acquire(f"pf{tc}")
                            p.bank_acquire(y2key(tc))
                            mm(ybank(tc)[:, 0:512],
                               [(WSM[:, slot * 2048 + kc * 256:slot * 2048 + kc * 256 + 128],
                                 xln[:, kc * TOK + tc * 512:kc * TOK + (tc + 1) * 512])
                                for kc in range(8)])
                            mm(y2bank(tc)[:, 0:512],
                               [(WSM[:, slot * 2048 + kc * 256 + 128:slot * 2048 + kc * 256 + 256],
                                 xln[:, kc * TOK + tc * 512:kc * TOK + (tc + 1) * 512])
                                for kc in range(8)])
                            tmv = p.t_mark(sT)
                            p.wait("scalar", sT, tmv)
                            p.wait("scalar", sV, tmp_guard[tc % 3])
                            p.S(lambda sc, tc=tc: sc.activation(
                                stmp[:, (tc % 3) * 512:(tc % 3 + 1) * 512],
                                ybank(tc)[:, 0:512], AF.Silu))
                            sdone = p.s_mark(sS)
                            p.wait("vector", sT, tmv)
                            p.V(lambda v, tc=tc: v.tensor_mul(
                                y2tmp[:, (tc % 3) * 512:(tc % 3 + 1) * 512],
                                y2bank(tc)[:, 0:512],
                                peb[:, tc * 512:(tc + 1) * 512]))
                            p.VD()
                            p.wait("vector", sS, sdone)
                            p.V(lambda v, fc=fc, tc=tc: v.tensor_mul(
                                g[:, fc * TOK + tc * 512:fc * TOK + (tc + 1) * 512],
                                stmp[:, (tc % 3) * 512:(tc % 3 + 1) * 512],
                                y2tmp[:, (tc % 3) * 512:(tc % 3 + 1) * 512]))
                            vmv = p.v_mark(sV)
                            tmp_guard[tc % 3] = vmv
                            p.bank_release(f"pf{tc}", sV, vmv)
                            p.bank_release(y2key(tc), sV, vmv)
                        chunk_done = sT.c
                        gi = e * 11 + fc
                        if gi + 4 < len(chunks):
                            ne, nfc = chunks[gi + 4]
                            load_w1(ne, nfc, slot_of[(ne, nfc)], chunk_done)
                    g_done = p.v_mark(sV)
                    if e + 1 < E:
                        p.wait("sync", sV, g_done)
                        for bt_ in range(2):
                            pe_vals[e + 1] = dma_y(peb[:, bt_ * 1024:(bt_ + 1) * 1024],
                                                   bass.AP(probs_hbm[ci], bt_ * 8192 + (e + 1) * 128,
                                                           [[0, 128], [1024, 8], [1, 128]]), dPE)
                    p.wait("tensor", sV, g_done)
                    p.wait("tensor", W2S, w2_vals[e])
                    for hc in range(8):
                        for tc in range(4):
                            r4 = (hc * 4 + tc) % 4
                            obank = pf[4 + r4] if r4 < 3 else pf7
                            okey = f"pf{4 + r4}" if r4 < 3 else "b7"
                            p.bank_acquire(okey)
                            def fo(t, hc=hc, tc=tc, obank=obank):
                                last = None
                                for fc in range(11):
                                    last = t.matmul(obank[:, 0:512],
                                                    WR1[:, fc * 1024 + hc * 128:fc * 1024 + (hc + 1) * 128],
                                                    g[:, fc * TOK + tc * 512:fc * TOK + (tc + 1) * 512],
                                                    start=(fc == 0), stop=(fc == 10))
                                return last
                            p.T(fo)
                            tmv = p.t_mark(sT)
                            p.wait("vector", sT, tmv)
                            p.V(lambda v, hc=hc, tc=tc, obank=obank: v.scalar_tensor_tensor(
                                x_fm[:, hc * TOK + tc * 512:hc * TOK + (tc + 1) * 512],
                                obank[:, 0:512], 1.0,
                                x_fm[:, hc * TOK + tc * 512:hc * TOK + (tc + 1) * 512],
                                op0=ALU.mult, op1=ALU.add))
                            vmv = p.v_mark(sV)
                            p.bank_release(okey, sV, vmv)
                    o_done = sT.c
                    if e + 1 < E:
                        load_w2(e + 1, o_done)
                res = p.v_mark(sV)
                return res, o_done

        def emit_attn_b(xln_ready, prev_tdone, wr1_free, wsm_free):
            vh = BIGA[:, 0:8704]
            ctx = BIGA[:, 8704:16896]
            Mi = BIGA[:, 16896:19136]
            Me0 = BIGA[:, 19136:21376]
            Me15 = BIGA[:, 21376:23616]
            xlh = BIGA[:, 23616:23712]
            with nc.psum_tensor("pb16b", [128, 1024], BF16) as pb16, \
                 nc.sbuf_tensor("ab_pads", [128, 2048], BF16) as pads, \
                 nc.sbuf_tensor("ab_q2", [128, TOK], BF16) as q2, \
                 nc.sbuf_tensor("ab_k2", [128, 2060], BF16) as k2, \
                 nc.sbuf_tensor("ab_cc", [128, 96], F32) as ccs, \
                 nc.sbuf_tensor("ab_s2", [128, 280], F32) as s2b, \
                 nc.sbuf_tensor("ab_ph", [128, 2 * 8 * 140], BF16) as ph, \
                 nc.sbuf_tensor("ab_pt", [128, 512], BF16) as pT, \
                 nc.sbuf_tensor("ab_dn", [128, 48], F32) as dn:
                pad0 = pads[:, 0:1024]
                pad16 = pads[:, 1024:2048]
                p.bank_acquire("b7")
                p.V(lambda v: v.memzero(pb16[:, 0:1024]))
                zv = p.v_mark(sV)
                p.bank_release("b7", sV, zv)
                p.wait("gpsimd", sT, prev_tdone)
                for h in range(H):
                    dma_g(Mi[:, h * 140:(h + 1) * 140], Mi_h.ap()[h], dM)
                    dma_g(Me0[:, h * 140:(h + 1) * 140], Me0_h.ap()[h], dM)
                    dma_g(Me15[:, h * 140:(h + 1) * 140], Me15_h.ap()[h], dM)
                m_done = dM.c

                p.wait("vector", sV, xln_ready)
                p.wait("vector", dSEL, sel_done)
                def fcc(v):
                    for kc in range(8):
                        v.tensor_single_scalar(ccs[:, kc * 12:kc * 12 + 6],
                                               xln[:, kc * TOK + 2042:kc * TOK + 2048],
                                               selbuf[:, 0:1], op=ALU.mult)
                        v.tensor_single_scalar(ccs[:, kc * 12 + 6:kc * 12 + 12],
                                               xln[:, kc * TOK + 0:kc * TOK + 6],
                                               selbuf[:, 1:2], op=ALU.mult)
                p.V(fcc)
                ccv = p.v_mark(sV)
                p.wait("gpsimd", sV, ccv)
                cc1 = dma_o(cc_in.ap(), ccs[:])
                p.wait("gpsimd", dO, cc1)
                ccg = sG.bump()
                p.G(lambda g_, s=sG.h: g_.collective_compute(
                    "AllReduce", ALU.add,
                    replica_groups=[[0, 1], [2, 3], [4, 5], [6, 7]],
                    ins=[cc_in.ap().opt()],
                    outs=[cc_out.ap().opt()]).then_inc(s))
                p.wait("gpsimd", sG, ccg)
                cc2 = dma_o(ccs[:], cc_out.ap())
                p.wait("vector", dO, cc2)
                p.wait("vector", sT, prev_tdone)
                def fxh(v):
                    for kc in range(8):
                        v.tensor_single_scalar(xlh[:, kc * 12:kc * 12 + 6],
                                               ccs[:, kc * 12:kc * 12 + 6],
                                               selbuf[:, 1:2], op=ALU.mult)
                        v.tensor_single_scalar(xlh[:, kc * 12 + 6:kc * 12 + 12],
                                               ccs[:, kc * 12 + 6:kc * 12 + 12],
                                               selbuf[:, 0:1], op=ALU.mult)
                p.V(fxh)
                p.VD()
                def fpads(v):
                    for kc in range(8):
                        v.tensor_copy(pad0[:, kc * 128:kc * 128 + 6], xlh[:, kc * 12:kc * 12 + 6])
                        v.tensor_copy(pad0[:, kc * 128 + 6:kc * 128 + 128],
                                      xln[:, kc * TOK:kc * TOK + 122])
                        v.tensor_copy(pad16[:, kc * 128:kc * 128 + 6],
                                      xln[:, kc * TOK + 2042:kc * TOK + 2048])
                        v.tensor_copy(pad16[:, kc * 128 + 6:kc * 128 + 12],
                                      xlh[:, kc * 12 + 6:kc * 12 + 12])
                        v.memset(pad16[:, kc * 128 + 12:kc * 128 + 128], 0.0)
                p.V(fpads)
                halo_done = p.v_mark(sV)

                qk_rel = [wsm_free, wsm_free]
                wv_guard = wr1_free
                last_res = 0
                ab_exp_marks = []
                ab_trg = [0, 0]
                for hf in range(2):
                    p.wait("gpsimd", sT, wv_guard)
                    p.wait("gpsimd", dWT2, dWT2.c)
                    wv_val = dma_g(WR1[:, 0:4096].rearrange("p (kc c) -> p kc c", kc=8),
                                   bass.AP(abWqkv_h, 2048 + hf * 512,
                                           [[3 * D, 128], [128 * 3 * D, 8], [1, 512]]),
                                   dWV)
                    p.wait("tensor", dWV, wv_val)
                    p.wait("tensor", dM, m_done)
                    for t16 in list(range(1, 16)) + [0, 16]:
                        ib = t16 % 2
                        p.bank_acquire(f"pf{ib}")
                        if t16 == 0:
                            p.wait("tensor", sV, halo_done)
                            lhs = lambda kc: pad0[:, kc * 128:(kc + 1) * 128]
                        elif t16 == 16:
                            lhs = lambda kc: pad16[:, kc * 128:(kc + 1) * 128]
                        else:
                            lhs = lambda kc, t16=t16: xln[:, kc * TOK + t16 * 128 - 6:kc * TOK + (t16 + 1) * 128 - 6]
                        mm(pf[ib][:, 0:512],
                           [(lhs(kc), WR1[:, kc * 512:(kc + 1) * 512]) for kc in range(8)],
                           extra=(ones_b[0:1, 0:128], abBv_row[0:1, hf * 512:(hf + 1) * 512]))
                        tmv = p.t_mark(sT)
                        p.wait("vector", sT, tmv)
                        p.V(lambda v, t16=t16, ib=ib: v.tensor_copy(
                            vh[:, t16 * 512:(t16 + 1) * 512], pf[ib][:, 0:512]))
                        vmv = p.v_mark(sV)
                        p.bank_release(f"pf{ib}", sV, vmv)
                    vh_done = p.v_mark(sV)
                    wv_guard = sT.c

                    ctx_marks = []
                    for prl in range(4):
                        pr = hf * 4 + prl
                        slot = prl % 2
                        p.wait("gpsimd", sT, qk_rel[slot])
                        p.wait("gpsimd", dWT1[slot], dWT1[slot].c)
                        base = slot * 2048
                        dma_g(WSM[:, base:base + 1024].rearrange("p (kc c) -> p kc c", kc=8),
                              bass.AP(abWqkv_h, pr * 128, [[3 * D, 128], [128 * 3 * D, 8], [1, 128]]),
                              dQK[slot])
                        qk_val = dma_g(WSM[:, base + 1024:base + 2048].rearrange("p (kc c) -> p kc c", kc=8),
                                       bass.AP(abWqkv_h, D + pr * 128, [[3 * D, 128], [128 * 3 * D, 8], [1, 128]]),
                                       dQK[slot])
                        p.wait("tensor", dQK[slot], qk_val)
                        for which in range(2):
                            for tc in range(4):
                                ib = 2 + tc % 2
                                p.bank_acquire(f"pf{ib}")
                                mm(pf[ib][:, 0:512],
                                   [(WSM[:, base + which * 1024 + kc * 128:base + which * 1024 + (kc + 1) * 128],
                                     xln[:, kc * TOK + tc * 512:kc * TOK + (tc + 1) * 512])
                                    for kc in range(8)])
                                tmv = p.t_mark(sT)
                                p.wait("vector", sT, tmv)
                                if which == 0:
                                    p.V(lambda v, tc=tc, ib=ib, pr=pr: v.tensor_scalar_add(
                                        q2[:, tc * 512:(tc + 1) * 512], pf[ib][:, 0:512],
                                        abB[:, pr:pr + 1]))
                                else:
                                    p.V(lambda v, tc=tc, ib=ib, pr=pr: v.tensor_scalar_add(
                                        k2[:, 6 + tc * 512:6 + (tc + 1) * 512], pf[ib][:, 0:512],
                                        abB[:, 8 + pr:9 + pr]))
                                vmv = p.v_mark(sV)
                                p.bank_release(f"pf{ib}", sV, vmv)
                        p.bank_acquire("pf4")
                        mm(pf[4][:, 0:12],
                           [(WSM[:, base + 1024 + kc * 128:base + 1024 + (kc + 1) * 128],
                             xlh[:, kc * 12:(kc + 1) * 12]) for kc in range(8)])
                        tmv = p.t_mark(sT)
                        p.wait("vector", sT, tmv)
                        p.V(lambda v, pr=pr: v.tensor_scalar_add(
                            k2[:, 0:6], pf[4][:, 0:6], abB[:, 8 + pr:9 + pr]))
                        p.V(lambda v, pr=pr: v.tensor_scalar_add(
                            k2[:, 2054:2060], pf[4][:, 6:12], abB[:, 8 + pr:9 + pr]))
                        vmv = p.v_mark(sV)
                        p.bank_release("pf4", sV, vmv)
                        qk_rel[slot] = sT.c
                        qkv_ready = p.v_mark(sV)
                        p.wait("tensor", sV, qkv_ready)
                        p.wait("tensor", sV, vh_done)

                        def ab_score(hh, cb, c8):
                            h = 2 * pr + hh
                            pb = (h % 2) * 64
                            ch = cb * 8 + c8
                            ib = c8 % 4
                            p.bank_acquire(f"pf{ib}")
                            p.T(lambda t, ib=ib, pb=pb, ch=ch: t.matmul(
                                pf[ib][:, 0:140],
                                q2[pb:pb + 64, ch * 128:(ch + 1) * 128],
                                k2[pb:pb + 64, ch * 128:ch * 128 + 140],
                                start=True, stop=True))
                            return p.t_mark(sT)

                        def ab_p1a(hh, cb):
                            return [ab_score(hh, cb, c8) for c8 in range(4)]

                        def ab_p1b(hh, cb, marks):
                            u = hh * 2 + cb
                            usl = u % 2
                            h = 2 * pr + hh
                            marks = list(marks)
                            exp_marks = list(ab_exp_marks)
                            for c8 in range(8):
                                ch = cb * 8 + c8
                                ib = c8 % 4
                                tmv = marks[c8]
                                p.wait("vector", sT, tmv)
                                if len(exp_marks) >= 2:
                                    p.wait("vector", sS, exp_marks[-2])
                                sc2 = s2b[:, (c8 % 2) * 140:(c8 % 2 + 1) * 140]
                                Msel = Me0 if ch == 0 else (Me15 if ch == 15 else Mi)
                                p.V(lambda v, ib=ib, h=h, sc2=sc2, Msel=Msel: v.scalar_tensor_tensor(
                                    sc2[:, :], pf[ib][:, 0:140], -SCALE,
                                    Msel[:, h * 140:(h + 1) * 140],
                                    op0=ALU.mult, op1=ALU.subtract))
                                p.VD()
                                p.V(lambda v, c8=c8, sc2=sc2, usl=usl: v.tensor_reduce(
                                    dn[:, usl * 24 + c8:usl * 24 + c8 + 1], sc2, axis=AX.X, op=ALU.min))
                                s2v = p.v_mark(sV)
                                p.bank_release(f"pf{ib}", sV, s2v)
                                if c8 + 4 < 8:
                                    marks.append(ab_score(hh, cb, c8 + 4))
                                p.wait("scalar", sV, s2v)
                                if ab_trg[usl]:
                                    p.wait("scalar", sT, ab_trg[usl])
                                p.S(lambda sc, c8=c8, sc2=sc2, usl=usl: sc.activation(
                                    ph[:, usl * 1120 + c8 * 140:usl * 1120 + (c8 + 1) * 140], sc2,
                                    AF.Exp, scale=-1.0, bias=dn[:, usl * 24 + c8:usl * 24 + c8 + 1],
                                    accum_out=dn[:, usl * 24 + 8 + c8:usl * 24 + 9 + c8]))
                                exp_marks.append(p.s_mark(sS))
                            ab_exp_marks[:] = exp_marks[-2:]
                            p.wait("vector", sS, exp_marks[-1])
                            p.V(lambda v, usl=usl: v.reciprocal(dn[:, usl * 24 + 16:usl * 24 + 24],
                                                                dn[:, usl * 24 + 8:usl * 24 + 16]))
                            p.VD()
                            def fnm(v, usl=usl):
                                for c8 in range(8):
                                    v.tensor_scalar_mul(ph[:, usl * 1120 + c8 * 140:usl * 1120 + (c8 + 1) * 140],
                                                        ph[:, usl * 1120 + c8 * 140:usl * 1120 + (c8 + 1) * 140],
                                                        dn[:, usl * 24 + 16 + c8:usl * 24 + 17 + c8])
                            p.V(fnm)
                            return p.v_mark(sV)

                        def ab_p2(hh, cb, nv):
                            u = hh * 2 + cb
                            usl = u % 2
                            h = 2 * pr + hh
                            pb = (h % 2) * 64
                            lh = h - hf * 8
                            p.wait("tensor", sV, nv)
                            p.bank_acquire("pf5")
                            for c8 in range(8):
                                ch = cb * 8 + c8
                                pts = (c8 % 2) * 256
                                if c8 % 2 == 0:
                                    p.bank_acquire("b7")
                                    def ftr(t, c8=c8, usl=usl):
                                        for j in range(2):
                                            t.transpose(pb16[:, j * 256:j * 256 + 128],
                                                        ph[:, usl * 1120 + (c8 + j) * 140:usl * 1120 + (c8 + j) * 140 + 128], eye_b[:])
                                            last = t.transpose(pb16[0:12, j * 256 + 128:j * 256 + 256],
                                                               ph[:, usl * 1120 + (c8 + j) * 140 + 128:usl * 1120 + (c8 + j) * 140 + 140], eye_b[:])
                                        return last
                                    p.T(ftr)
                                    tmv = p.t_mark(sT)
                                    p.wait("vector", sT, tmv)
                                    p.V(lambda v: v.tensor_copy(pT[:, 0:512], pb16[:, 0:512]))
                                    ptv = p.v_mark(sV)
                                    p.bank_release("b7", sV, ptv)
                                    p.wait("tensor", sV, ptv)
                                def fctx(t, ch=ch, lh=lh, pb=pb, c8=c8, pts=pts):
                                    t.matmul(pf[5][pb:pb + 64, (c8 % 4) * 128:(c8 % 4 + 1) * 128],
                                             vh[:, ch * 512 + lh * 64:ch * 512 + (lh + 1) * 64],
                                             pT[:, pts:pts + 128], start=True, stop=False)
                                    return t.matmul(pf[5][pb:pb + 64, (c8 % 4) * 128:(c8 % 4 + 1) * 128],
                                                    vh[0:12, (ch + 1) * 512 + lh * 64:(ch + 1) * 512 + (lh + 1) * 64],
                                                    pT[0:12, pts + 128:pts + 256], start=False, stop=True)
                                p.T(fctx)
                                tctx = p.t_mark(sT)
                                if c8 % 4 == 3:
                                    p.wait("vector", sT, tctx)
                                    ch0 = cb * 8 + c8 - 3
                                    p.V(lambda v, pb=pb, lh=lh, ch0=ch0: v.tensor_copy(
                                        ctx[pb:pb + 64, (lh // 2) * TOK + ch0 * 128:(lh // 2) * TOK + (ch0 + 4) * 128],
                                        pf[5][pb:pb + 64, 0:512]))
                                    cvv = p.v_mark(sV)
                                    p.bank_release("pf5", sV, cvv)
                                    if c8 < 7:
                                        p.bank_acquire("pf5")
                            ab_trg[usl] = sT.c

                        pend = []
                        for hh in range(2):
                            for cb in range(2):
                                marks = ab_p1a(hh, cb)
                                if pend:
                                    ab_p2(*pend.pop(0))
                                nv = ab_p1b(hh, cb, marks)
                                pend.append((hh, cb, nv))
                        while pend:
                            ab_p2(*pend.pop(0))
                        ctx_marks.append(sV.c)
                    ctx_done = p.v_mark(sV)
                    pj_guard = sT.c
                    p.wait("gpsimd", sT, pj_guard)
                    pj_val = dma_g(WR1[:, 0:4096].rearrange("p (kc c) -> p kc c", kc=8),
                                   bass.AP(abProjW_h, hf * 512 * D,
                                           [[D, 128], [128 * D, 4], [1, D]]),
                                   dPJ)
                    p.wait("tensor", dPJ, pj_val)
                    p.wait("tensor", sV, ctx_done)
                    for hc in range(8):
                        for tc in range(4):
                            ib = 2 + (hc * 4 + tc) % 2
                            p.bank_acquire(f"pf{ib}")
                            mm(pf[ib][:, 0:512],
                               [(WR1[:, kc4 * 1024 + hc * 128:kc4 * 1024 + (hc + 1) * 128],
                                 ctx[:, kc4 * TOK + tc * 512:kc4 * TOK + (tc + 1) * 512])
                                for kc4 in range(4)])
                            tmv = p.t_mark(sT)
                            p.wait("vector", sT, tmv)
                            if hf == 0:
                                p.V(lambda v, hc=hc, tc=tc, ib=ib: v.scalar_tensor_tensor(
                                    x_fm[:, hc * TOK + tc * 512:hc * TOK + (tc + 1) * 512],
                                    pf[ib][:, 0:512], 1.0,
                                    x_fm[:, hc * TOK + tc * 512:hc * TOK + (tc + 1) * 512],
                                    op0=ALU.mult, op1=ALU.add))
                            else:
                                p.V(lambda v, hc=hc, tc=tc, ib=ib: v.scalar_tensor_tensor(
                                    x_fm[:, hc * TOK + tc * 512:hc * TOK + (tc + 1) * 512],
                                    pf[ib][:, 0:512], abProjB[:, hc:hc + 1],
                                    x_fm[:, hc * TOK + tc * 512:hc * TOK + (tc + 1) * 512],
                                    op0=ALU.add, op1=ALU.add))
                            last_res = p.v_mark(sV)
                            p.bank_release(f"pf{ib}", sV, last_res)
                    wv_guard = sT.c
                return last_res, sT.c

        dep = [(sV, x_ready)]
        tdone = 0
        wr1_free = 0
        wsm_free = 0
        if n_sub >= 1:
            lr = emit_ln(dep)
            m, tdone = emit_attn_a(lr, wr1_free)
            dep = [(sV, m)]
            if 0 in taps:
                p.wait("sync", sV, m)
                dma_y(dbg_h[0].ap(), x_fm[:])
        if n_sub >= 2:
            lr = emit_ln(dep, want_stt=True)
            m, tdone = emit_ffn(0, lr, tdone, tdone, tdone)
            dep = [(sV, m)]
            if 1 in taps:
                p.wait("sync", sV, m)
                dma_y(dbg_h[1].ap(), x_fm[:])
        if n_sub >= 3:
            lr = emit_ln(dep)
            m, tdone = emit_attn_b(lr, tdone, tdone, tdone)
            dep = [(sV, m)]
            if 2 in taps:
                p.wait("sync", sV, m)
                dma_y(dbg_h[2].ap(), x_fm[:])
        if n_sub >= 4:
            lr = emit_ln(dep, want_stt=True)
            m, tdone = emit_ffn(1, lr, tdone, tdone, tdone)
            dep = [(sV, m)]
            if 3 in taps:
                p.wait("sync", sV, m)
                dma_y(dbg_h[3].ap(), x_fm[:])

        with nc.sbuf_tensor("ostage", [128, 4 * D], F32) as ostage:
            for sem, val in dep:
                p.wait("tensor", sem, val)
            for rd in range(4):
                if rd > 0:
                    p.wait("vector", dO, dO.c)
                for i in range(4):
                    ti = rd * 4 + i
                    for kcg in range(2):
                        ib = (i * 2 + kcg) % 4
                        p.bank_acquire(f"pf{ib}")
                        def fot(t, ti=ti, kcg=kcg, ib=ib):
                            last = None
                            for j in range(4):
                                kc = kcg * 4 + j
                                last = t.transpose(pf[ib][:, j * 128:(j + 1) * 128],
                                                   x_fm[:, kc * TOK + ti * 128:kc * TOK + (ti + 1) * 128],
                                                   eye_f[:])
                            return last
                        p.T(fot)
                        tmv = p.t_mark(sT)
                        p.wait("vector", sT, tmv)
                        p.V(lambda v, i=i, kcg=kcg, ib=ib: v.tensor_copy(
                            ostage[:, i * D + kcg * 512:i * D + (kcg + 1) * 512],
                            pf[ib][:, 0:512]))
                        vmv = p.v_mark(sV)
                        p.bank_release(f"pf{ib}", sV, vmv)
                ov = p.v_mark(sV)
                p.wait("gpsimd", sV, ov)
                for i in range(4):
                    ti = rd * 4 + i
                    dma_o(out_h.ap()[ti * 128:(ti + 1) * 128, :],
                          ostage[:, i * D:(i + 1) * D])
        p.wait("gpsimd", dO, dO.c)

        with nc.Block() as block:
            @block.tensor
            def _(t):
                for fn in p.q["tensor"]:
                    fn(t)

            @block.vector
            def _(v):
                for fn in p.q["vector"]:
                    fn(v)

            @block.scalar
            def _(s):
                for fn in p.q["scalar"]:
                    fn(s)

            @block.gpsimd
            def _(g):
                for fn in p.q["gpsimd"]:
                    fn(g)

            @block.sync
            def _(s):
                for fn in p.q["sync"]:
                    fn(s)

    return nc


def _build_M(rpb, seg):
    l = np.arange(128)[:, None]
    c = np.arange(140)[None, :]
    bidx = np.clip(6 + c - l, 0, 24)
    gath = np.take(rpb, bidx.reshape(-1), axis=1).reshape(rpb.shape[0], 128, 140)
    valid_i = (c - l >= 0) & (c - l <= 12)
    Mi = np.where(valid_i[None], gath, np.float32(NEG))
    Me0 = Mi.copy()
    Me15 = Mi.copy()
    if seg == 0:
        v0 = (c >= 6) & (c <= 18)
        patch = np.where(v0[None], gath, np.float32(NEG))
        Me0[:, 0:6, :] = patch[:, 0:6, :]
    else:
        v15 = (c >= 121) & (c <= 133)
        patch = np.where(v15[None], gath, np.float32(NEG))
        Me15[:, 122:128, :] = patch[:, 122:128, :]
    return (np.ascontiguousarray(Mi.astype(np.float32)),
            np.ascontiguousarray(Me0.astype(np.float32)),
            np.ascontiguousarray(Me15.astype(np.float32)))


def kernel(L0_attn_a_X, L0_attn_a_Wq, L0_attn_a_bq, L0_attn_a_Wkv, L0_attn_a_bkv,
           L0_attn_b_qkv_w, L0_attn_b_qkv_b, L0_attn_b_rpb, L0_attn_b_proj_w,
           L0_attn_b_proj_b, L0_ffn_router_weight, L0_ffn_w1, L0_ffn_w2):
    key = (N_SUB, tuple(DEBUG_TAPS))
    if key not in _CACHE:
        _CACHE[key] = build_program(N_SUB, tuple(DEBUG_TAPS))
    nc = _CACHE[key]

    X = np.asarray(L0_attn_a_X, dtype=np.float32)
    eye = np.eye(128, dtype=np.float32)
    eye2 = np.ascontiguousarray(np.concatenate([np.eye(64, dtype=np.float32)] * 2, axis=0))
    shared = dict(
        aaWq=np.ascontiguousarray(np.asarray(L0_attn_a_Wq, np.float32)),
        aaWkv=np.ascontiguousarray(np.asarray(L0_attn_a_Wkv, np.float32)),
        aaBq=np.ascontiguousarray(np.asarray(L0_attn_a_bq, np.float32).reshape(8, 128).T),
        aaBkv=np.ascontiguousarray(np.asarray(L0_attn_a_bkv, np.float32).reshape(16, 128).T),
        abWqkv=np.ascontiguousarray(np.asarray(L0_attn_b_qkv_w, np.float32).T),
        abB=np.ascontiguousarray(np.asarray(L0_attn_b_qkv_b, np.float32).reshape(24, 128).T),
        abBv_row=np.ascontiguousarray(np.asarray(L0_attn_b_qkv_b, np.float32)[2048:3072].reshape(1, 1024)),
        abProjW=np.ascontiguousarray(np.asarray(L0_attn_b_proj_w, np.float32).T),
        abProjB=np.ascontiguousarray(np.asarray(L0_attn_b_proj_b, np.float32).reshape(8, 128).T),
        fRouter=np.ascontiguousarray(np.asarray(L0_ffn_router_weight, np.float32)),
        fW1=np.ascontiguousarray(np.asarray(L0_ffn_w1, np.float32)),
        fW2=np.ascontiguousarray(np.asarray(L0_ffn_w2, np.float32)),
        eye=eye, eye2=eye2,
    )
    rpb = np.asarray(L0_attn_b_rpb, np.float32)
    Ms = {s: _build_M(rpb, s) for s in (0, 1)}
    in_maps = []
    for c in range(8):
        b, seg = c // 2, c % 2
        m = dict(shared)
        m["x0"] = np.ascontiguousarray(X[b, seg * TOK:(seg + 1) * TOK])
        m["Mi"], m["Me0"], m["Me15"] = Ms[seg]
        m["sel"] = np.asarray([[1.0, 0.0]] if seg == 0 else [[0.0, 1.0]], np.float32)
        in_maps.append(m)

    res = run_bass_kernel_spmd(nc, in_maps, core_ids=list(range(8)))
    out = np.empty((4, 4096, 1024), np.float32)
    for c in range(8):
        b, seg = c // 2, c % 2
        out[b, seg * TOK:(seg + 1) * TOK] = res.results[c]["out"]
    if DEBUG_TAPS:
        kernel.last_dbg = [{i: r[f"dbg{i}"] for i in DEBUG_TAPS} for r in res.results]
    return out


# version 12 (final)
# speedup vs baseline: 1.0830x; 1.0830x over previous
import numpy as np
from contextlib import ExitStack

import concourse.bass as bass
import concourse.mybir as mybir
from concourse.bass_utils import run_bass_kernel_spmd

F32 = mybir.dt.float32
BF16 = mybir.dt.bfloat16
AF = mybir.ActivationFunctionType
ALU = mybir.AluOpType
AX = mybir.AxisListType

D = 1024
TOK = 2048
H = 16
SCALE = 0.125
E = 8
FF = 1408
NEG = -1.0e30

N_SUB = 4
DEBUG_TAPS = []
_CACHE = {}


class Sem:
    def __init__(self, h):
        self.h = h
        self.c = 0

    def bump(self, n=1):
        self.c += n
        return self.c


class Prog:
    def __init__(self):
        self.q = {e: [] for e in ("tensor", "vector", "scalar", "gpsimd", "sync")}
        self.bank_free = {}
        self.last_t = [None, False]

    def T(self, fn):
        def w(t, fn=fn):
            r = fn(t)
            if r is not None:
                self.last_t[0] = r
                self.last_t[1] = False
        self.q["tensor"].append(w)
    def V(self, fn): self.q["vector"].append(fn)
    def S(self, fn): self.q["scalar"].append(fn)
    def G(self, fn): self.q["gpsimd"].append(fn)
    def Y(self, fn): self.q["sync"].append(fn)

    def wait(self, eng, sem, val):
        if val <= 0:
            return
        self.q[eng].append(lambda e, s=sem.h, v=val: e.wait_ge(s, v))

    def t_mark(self, sem):
        val = sem.bump()
        def w(t, s=sem.h):
            inst, marked = self.last_t
            if inst is not None and not marked:
                inst.then_inc(s, 1)
                self.last_t[1] = True
            else:
                t.drain().then_inc(s, 1)
        self.q["tensor"].append(w)
        return val

    def v_mark(self, sem):
        val = sem.bump()
        self.V(lambda v, s=sem.h: v.drain().then_inc(s, 1))
        return val

    def s_mark(self, sem):
        val = sem.bump()
        self.S(lambda v, s=sem.h: v.drain().then_inc(s, 1))
        return val

    def VD(self):
        self.q["vector"].append(lambda v: v.drain())

    def SD(self):
        self.q["scalar"].append(lambda v: v.drain())

    def bank_acquire(self, bank):
        rel = self.bank_free.get(bank)
        if rel is not None:
            sem, val = rel
            self.wait("tensor", sem, val)
            self.bank_free[bank] = None

    def bank_release(self, bank, sem, val):
        self.bank_free[bank] = (sem, val)


def build_program(n_sub=N_SUB, taps=()):
    taps = list(taps)
    nc = bass.Bass("TRN2", target_bir_lowering=False)

    din = lambda name, shape, dt=F32: nc.dram_tensor(name, list(shape), dt, kind="ExternalInput")
    x0_h = din("x0", (TOK, D))
    aaWq_h = din("aaWq", (D, D))
    aaWkv_h = din("aaWkv", (D, 2 * D))
    aaBq_h = din("aaBq", (128, 8))
    aaBkv_h = din("aaBkv", (128, 16))
    abWqkv_h = din("abWqkv", (D, 3 * D))
    abB_h = din("abB", (128, 24))
    abBv_row_h = din("abBv_row", (1, D))
    abProjW_h = din("abProjW", (D, D))
    abProjB_h = din("abProjB", (128, 8))
    fRouter_h = din("fRouter", (D, E))
    fW1_h = din("fW1", (E, D, 2 * FF))
    fW2_h = din("fW2", (E, FF, D))
    eye_h = din("eye", (128, 128))
    eye2_h = din("eye2", (128, 64))
    Mi_h = din("Mi", (H, 128, 140))
    Me0_h = din("Me0", (H, 128, 140))
    Me15_h = din("Me15", (H, 128, 140))
    sel_h = din("sel", (1, 2))

    out_h = nc.dram_tensor("out", [TOK, D], F32, kind="ExternalOutput")
    w1bf_h = nc.dram_tensor("w1bf", [E, D, 2 * FF], BF16, kind="Internal")
    w2bf_h = nc.dram_tensor("w2bf", [E, FF, D], BF16, kind="Internal")
    probs_hbm = [nc.dram_tensor(f"probsh{i}", [E, TOK], F32, kind="Internal") for i in range(2)]
    cc_in = nc.dram_tensor("cc_in", [128, 96], F32, kind="Internal")
    cc_out = nc.dram_tensor("cc_out", [128, 96], F32, kind="Internal")
    dbg_h = {i: nc.dram_tensor(f"dbg{i}", [128, 8 * TOK], F32, kind="ExternalOutput")
             for i in taps}

    p = Prog()
    es = ExitStack()
    with es:
        sbuf = lambda name, shape, dt: es.enter_context(nc.sbuf_tensor(name, list(shape), dt))
        psum = lambda name, shape, dt: es.enter_context(nc.psum_tensor(name, list(shape), dt))
        mksem = lambda name: Sem(es.enter_context(nc.semaphore(name)))

        sT = mksem("sT")
        sV = mksem("sV")
        sS = mksem("sS")
        sG = mksem("sG")
        dW = mksem("dW")
        dY = mksem("dY")
        dO = mksem("dO")
        dCONST = mksem("dCONST")
        dX = mksem("dX")
        dSEL = mksem("dSEL")
        dWV = mksem("dWV")
        dQK = [mksem("dQK0"), mksem("dQK1")]
        dM = mksem("dM")
        dPJ = mksem("dPJ")
        dRW = mksem("dRW")
        dW1 = [mksem(f"dW1_{i}") for i in range(4)]
        dW2s = mksem("dW2")
        dPE = mksem("dPE")
        dPROBS = mksem("dPROBS")
        dWT1 = [mksem(f"dWT1_{i}") for i in range(4)]
        dWT2 = mksem("dWT2")
        dW1s2 = [mksem(f"dW1s2_{i}") for i in range(4)]
        dW2ss2 = mksem("dW2ss2")

        x_fm = sbuf("x_fm", (128, 8 * TOK), F32)
        xln = sbuf("xln", (128, 8 * TOK), BF16)
        BIGA = sbuf("BIGA", (128, 23712), BF16)
        WR1 = sbuf("WR1", (128, 11264), BF16)
        WSM = sbuf("WSM", (128, 8192), BF16)
        ones_b = sbuf("ones_b", (128, 128), BF16)
        eye_b = sbuf("eye_b", (128, 128), BF16)
        eyen_b = sbuf("eyen_b", (128, 128), BF16)
        eye_f = sbuf("eye_f", (128, 128), F32)
        eye2_b = sbuf("eye2_b", (128, 64), BF16)
        aaBq = sbuf("aaBq_s", (128, 8), F32)
        aaBkv = sbuf("aaBkv_s", (128, 16), F32)
        abB = sbuf("abB_s", (128, 24), F32)
        abBv_row = sbuf("abBv_row_s", (1, D), BF16)
        abProjB = sbuf("abProjB_s", (128, 8), F32)
        selbuf = sbuf("selbuf", (128, 2), F32)
        epsb = sbuf("epsb", (1, 1), F32)
        ones_f = sbuf("ones_f", (128, 128), F32)
        sttbuf = sbuf("sttbuf", (128, 32), F32)

        pf = [psum(f"pf{i}", (128, 512), F32) for i in range(7)]

        def dma_g(out_ap, in_ap, sem=None):
            sem = sem or dW
            val = sem.bump(16)
            p.G(lambda e, o=out_ap, i=in_ap, s=sem.h: e.dma_start(o, i).then_inc(s, 16))
            return val

        def dma_y(out_ap, in_ap, sem=None):
            sem = sem or dY
            val = sem.bump(16)
            p.Y(lambda e, o=out_ap, i=in_ap, s=sem.h: e.dma_start(o, i).then_inc(s, 16))
            return val

        def dma_o(out_ap, in_ap, sem=None):
            sem = sem or dO
            val = sem.bump(16)
            p.G(lambda e, o=out_ap, i=in_ap, s=sem.h: e.dma_start(o, i).then_inc(s, 16))
            return val

        def mm(out_ap, pairs, extra=None):
            def f(t, out_ap=out_ap, pairs=list(pairs), extra=extra):
                n = len(pairs)
                last = None
                for i, (l, r) in enumerate(pairs):
                    last = t.matmul(out_ap, l, r, start=(i == 0),
                                    stop=(i == n - 1 and extra is None))
                if extra is not None:
                    last = t.matmul(out_ap, extra[0], extra[1], start=False, stop=True)
                return last
            p.T(f)

        dma_g(eye_f[:], eye_h.ap(), dCONST)
        dma_g(eye_b[:], eye_h.ap(), dCONST)
        dma_g(eye2_b[:], eye2_h.ap(), dCONST)
        dma_g(aaBq[:], aaBq_h.ap(), dCONST)
        dma_g(aaBkv[:], aaBkv_h.ap(), dCONST)
        dma_g(abB[:], abB_h.ap(), dCONST)
        dma_g(abBv_row[:], abBv_row_h.ap(), dCONST)
        dma_g(abProjB[:], abProjB_h.ap(), dCONST)
        consts_done = dCONST.c
        p.wait("vector", dCONST, consts_done)
        p.V(lambda v: v.tensor_scalar_mul(eyen_b[:], eye_b[:], -1.0))
        p.V(lambda v: v.memset(ones_b[:], 1.0))
        p.V(lambda v: v.memset(ones_f[:], 1.0))
        p.V(lambda v: v.memset(epsb[:], 1.0e-5))

        ones_done = p.v_mark(sV)
        dma_y(selbuf[:], bass.AP(sel_h, 0, [[0, 128], [1, 2]]), dSEL)
        sel_done = dSEL.c

        with nc.sbuf_tensor("xstage", [128, 4 * D], F32) as xstage:
            for rd in range(4):
                for i in range(4):
                    ti = rd * 4 + i
                    dma_y(xstage[:, i * D:(i + 1) * D],
                          x0_h.ap()[ti * 128:(ti + 1) * 128, :], dX)
                p.wait("tensor", dX, dX.c)
                p.wait("tensor", dCONST, consts_done)
                for kc in range(8):
                    bank = f"pf{kc % 4}"
                    p.bank_acquire(bank)
                    def ftr(t, kc=kc):
                        last = None
                        for i in range(4):
                            last = t.transpose(pf[kc % 4][:, i * 128:(i + 1) * 128],
                                               xstage[:, i * D + kc * 128:i * D + (kc + 1) * 128],
                                               eye_f[:])
                        return last
                    p.T(ftr)
                    tmv = p.t_mark(sT)
                    p.wait("vector", sT, tmv)
                    p.V(lambda v, rd=rd, kc=kc: v.tensor_copy(
                        x_fm[:, kc * TOK + rd * 512:kc * TOK + (rd + 1) * 512],
                        pf[kc % 4][:, 0:512]))
                    vmv = p.v_mark(sV)
                    p.bank_release(bank, sV, vmv)
                if rd < 3:
                    tmx = p.t_mark(sT)
                    p.wait("sync", sT, tmx)
        x_ready = p.v_mark(sV)

        _ln_n = [0]

        def emit_ln(dep, want_stt=False):
            bw0 = lambda: BIGA[:, 0:TOK]
            bw1 = lambda: BIGA[:, TOK:2 * TOK]
            _ln_n[0] += 1
            nn = _ln_n[0]
            with nc.sbuf_tensor(f"ln_row{nn}", [1, 3072], F32) as lnrow, \
                 nc.sbuf_tensor(f"ln_m{nn}", [1, TOK], BF16) as row_m, \
                 nc.sbuf_tensor(f"ln_r{nn}", [1, TOK], BF16) as row_r:
                for sem, val in dep:
                    p.wait("scalar", sem, val)
                    p.wait("vector", sem, val)
                    p.wait("tensor", sem, val)
                for pt in range(8):
                    p.S(lambda sc, pt=pt: sc.activation(
                        xln[:, pt * TOK:(pt + 1) * TOK],
                        x_fm[:, pt * TOK:(pt + 1) * TOK], AF.Copy))
                cast_done = p.s_mark(sS)
                p.wait("vector", sS, cast_done)
                p.wait("tensor", sS, cast_done)
                p.wait("tensor", sV, ones_done)
                for tc in range(4):
                    def fsq(v, tc=tc):
                        for pt in range(8):
                            sl = xln[:, pt * TOK + tc * 512:pt * TOK + (tc + 1) * 512]
                            v.tensor_mul(BIGA[:, 4096 + pt * 512:4096 + (pt + 1) * 512], sl, sl)
                    p.V(fsq)
                    sqv = p.v_mark(sV)
                    p.wait("tensor", sV, sqv)
                    p.bank_acquire("pf4")
                    p.bank_acquire("pf5")
                    def fmm(t, tc=tc):
                        for pt in range(8):
                            t.matmul(pf[4][0:1, 0:512], ones_b[:, 0:1],
                                     xln[:, pt * TOK + tc * 512:pt * TOK + (tc + 1) * 512],
                                     start=(pt == 0), stop=(pt == 7))
                        last = None
                        for pt in range(8):
                            last = t.matmul(pf[5][0:1, 0:512], ones_b[:, 0:1],
                                            BIGA[:, 4096 + pt * 512:4096 + (pt + 1) * 512],
                                            start=(pt == 0), stop=(pt == 7))
                        return last
                    p.T(fmm)
                    tmv = p.t_mark(sT)
                    p.wait("vector", sT, tmv)
                    c = tc * 512
                    p.V(lambda v, c=c: v.tensor_scalar_mul(lnrow[0:1, 0:512], pf[4][0:1, 0:512], 1.0 / 1024.0))
                    p.V(lambda v, c=c: v.tensor_scalar_mul(row_m[0:1, c:c + 512], pf[4][0:1, 0:512], -1.0 / 1024.0))
                    p.VD()
                    p.V(lambda v: v.tensor_mul(lnrow[0:1, 512:1024], lnrow[0:1, 0:512], lnrow[0:1, 0:512]))
                    p.VD()
                    p.V(lambda v: v.scalar_tensor_tensor(
                        lnrow[0:1, 1024:1536], pf[5][0:1, 0:512], 1.0 / 1024.0,
                        lnrow[0:1, 512:1024], op0=ALU.mult, op1=ALU.subtract))
                    if want_stt:
                        p.V(lambda v: v.tensor_scalar_mul(lnrow[0:1, 2048:2560],
                                                          lnrow[0:1, 0:512], -1.0))
                    stv = p.v_mark(sV)
                    p.bank_release("pf4", sV, stv)
                    p.bank_release("pf5", sV, stv)
                    p.wait("scalar", sV, stv)
                    p.S(lambda sc: sc.activation(lnrow[0:1, 1536:2048], lnrow[0:1, 1024:1536],
                                                 AF.Ln, bias=epsb[0:1, 0:1]))
                    p.SD()
                    p.S(lambda sc, c=c: sc.activation(row_r[0:1, c:c + 512], lnrow[0:1, 1536:2048],
                                                      AF.Exp, scale=-0.5))
                    if want_stt:
                        p.S(lambda sc: sc.activation(lnrow[0:1, 2560:3072], lnrow[0:1, 1536:2048],
                                                     AF.Exp, scale=-0.5))
                    lns = p.s_mark(sS)
                    p.wait("vector", sS, lns)
                    if want_stt:
                        if tc == 0:
                            p.bank_acquire("pf6")
                        p.wait("tensor", sS, lns)
                        p.wait("tensor", sV, stv)
                        def fst(t, tc=tc):
                            last = None
                            for j in range(4):
                                last = t.transpose(pf[6][:, tc * 4 + j:tc * 4 + j + 1],
                                                   lnrow[0:1, 2048 + j * 128:2048 + (j + 1) * 128],
                                                   eye_f[0:1, 0:1])
                                last = t.transpose(pf[6][:, 16 + tc * 4 + j:16 + tc * 4 + j + 1],
                                                   lnrow[0:1, 2560 + j * 128:2560 + (j + 1) * 128],
                                                   eye_f[0:1, 0:1])
                            return last
                        p.T(fst)
                        stt_t = p.t_mark(sT)
                        if tc == 3:
                            p.wait("vector", sT, stt_t)
                            p.V(lambda v: v.tensor_copy(sttbuf[:, 0:32], pf[6][:, 0:32]))
                            sttv = p.v_mark(sV)
                            p.bank_release("pf6", sV, sttv)
                rdone = p.s_mark(sS)
                p.wait("tensor", sS, rdone)
                for tc in range(4):
                    c = tc * 512
                    p.bank_acquire("pf4")
                    p.bank_acquire("pf5")
                    p.T(lambda t, c=c: t.matmul(pf[4][:, 0:512], ones_b[0:1, 0:128],
                                                row_m[0:1, c:c + 512], start=True, stop=True))
                    p.T(lambda t, c=c: t.matmul(pf[5][:, 0:512], ones_b[0:1, 0:128],
                                                row_r[0:1, c:c + 512], start=True, stop=True))
                    tmb = p.t_mark(sT)
                    p.wait("vector", sT, tmb)
                    p.V(lambda v, c=c: v.tensor_copy(bw0()[:, c:c + 512], pf[4][:, 0:512]))
                    p.V(lambda v, c=c: v.tensor_copy(bw1()[:, c:c + 512], pf[5][:, 0:512]))
                    bmv = p.v_mark(sV)
                    p.bank_release("pf4", sV, bmv)
                    p.bank_release("pf5", sV, bmv)
                def fno_a(v):
                    for pt in range(8):
                        sl = xln[:, pt * TOK:(pt + 1) * TOK]
                        v.tensor_add(sl, sl, bw0())
                p.V(fno_a)
                p.VD()
                def fno_b(v):
                    for pt in range(8):
                        sl = xln[:, pt * TOK:(pt + 1) * TOK]
                        v.tensor_mul(sl, sl, bw1())
                p.V(fno_b)
                return p.v_mark(sV)

        def emit_attn_a(xln_ready, wr1_free):
            va = BIGA[:, 0:16384]
            probs = BIGA[:, 16384:18432]
            probsT = BIGA[:, 18432:20480]
            q2 = BIGA[:, 20480:22528]
            k2 = WSM[:, 0:2048]
            sv = WSM[:, 2048:2304]
            with nc.sbuf_tensor("aa_st", [128, 12], F32) as st, \
                 nc.psum_tensor("pb16a", [128, 1024], BF16) as pb16:
                p.bank_acquire("b7")
                p.wait("vector", sV, ones_done)
                p.V(lambda v: v.memzero(pb16[:, 0:1024]))
                zv = p.v_mark(sV)
                p.bank_release("b7", sV, zv)
                p.wait("gpsimd", sT, wr1_free)
                wv_val = dma_g(
                    WR1[:, 0:8192].rearrange("p (kc c) -> p kc c", kc=8),
                    bass.AP(aaWkv_h, 1024, [[2 * D, 128], [128 * 2 * D, 8], [1, 1024]]),
                    dWV)
                p.wait("tensor", dWV, wv_val)
                p.wait("tensor", sV, xln_ready)
                for vc in range(8):
                    for tc in range(4):
                        ib = (vc * 4 + tc) % 4
                        bank = f"pf{ib}"
                        p.bank_acquire(bank)
                        mm(pf[ib][:, 0:512],
                           [(WR1[:, kc * 1024 + vc * 128:kc * 1024 + (vc + 1) * 128],
                             xln[:, kc * TOK + tc * 512:kc * TOK + (tc + 1) * 512])
                            for kc in range(8)])
                        tmv = p.t_mark(sT)
                        p.wait("vector", sT, tmv)
                        p.V(lambda v, vc=vc, tc=tc, ib=ib: v.tensor_scalar_add(
                            va[:, vc * TOK + tc * 512:vc * TOK + (tc + 1) * 512],
                            pf[ib][:, 0:512], aaBkv[:, 8 + vc:9 + vc]))
                        vmv = p.v_mark(sV)
                        p.bank_release(bank, sV, vmv)
                wv_used = sT.c

                qk_rel = [wv_used, wv_used]
                last_res = 0
                for pr in range(8):
                    slot = pr % 2
                    p.wait("gpsimd", sT, qk_rel[slot])
                    base = slot * 2048
                    dma_g(WR1[:, base:base + 1024].rearrange("p (kc c) -> p kc c", kc=8),
                          bass.AP(aaWq_h, pr * 128, [[D, 128], [128 * D, 8], [1, 128]]),
                          dQK[slot])
                    qk_val = dma_g(WR1[:, base + 1024:base + 2048].rearrange("p (kc c) -> p kc c", kc=8),
                                   bass.AP(aaWkv_h, pr * 128, [[2 * D, 128], [128 * 2 * D, 8], [1, 128]]),
                                   dQK[slot])
                    p.wait("tensor", dQK[slot], qk_val)
                    for which, bias_ap in ((0, aaBq[:, pr:pr + 1]), (1, aaBkv[:, pr:pr + 1])):
                        dst = q2 if which == 0 else k2
                        for tc in range(4):
                            ib = 4 + tc % 2
                            bank = f"pf{ib}"
                            p.bank_acquire(bank)
                            mm(pf[ib][:, 0:512],
                               [(WR1[:, base + which * 1024 + kc * 128:base + which * 1024 + (kc + 1) * 128],
                                 xln[:, kc * TOK + tc * 512:kc * TOK + (tc + 1) * 512])
                                for kc in range(8)])
                            tmv = p.t_mark(sT)
                            p.wait("vector", sT, tmv)
                            if which == 0:
                                p.V(lambda v, dst=dst, b=bias_ap, tc=tc, ib=ib: v.tensor_scalar(
                                    dst[:, tc * 512:(tc + 1) * 512], pf[ib][:, 0:512],
                                    b, -SCALE, op0=ALU.add, op1=ALU.mult))
                            else:
                                p.V(lambda v, dst=dst, b=bias_ap, tc=tc, ib=ib: v.tensor_scalar_add(
                                    dst[:, tc * 512:(tc + 1) * 512], pf[ib][:, 0:512], b))
                            vmv = p.v_mark(sV)
                            p.bank_release(bank, sV, vmv)
                    qk_rel[slot] = sT.c
                    qkv_ready = p.v_mark(sV)
                    p.wait("tensor", sV, qkv_ready)

                    for hh in range(2):
                        h = 2 * pr + hh
                        pb = (h % 2) * 64
                        r = h % 4
                        vblk = h // 2
                        sc_marks = []
                        for qc in range(4):
                            p.bank_acquire(f"pf{qc}")
                            p.T(lambda t, qc=qc, pb=pb, r=r: t.matmul(
                                pf[qc][:, 0:512],
                                q2[pb:pb + 64, 512 * qc + r:512 * qc + r + 509:4],
                                k2[pb:pb + 64, r:r + 2045:4],
                                start=True, stop=True))
                            sc_marks.append(p.t_mark(sT))
                        for qc in range(4):
                            p.wait("vector", sT, sc_marks[qc])
                            p.V(lambda v, qc=qc: v.tensor_reduce(st[:, 4 + qc:5 + qc],
                                                                 pf[qc][:, 0:512], axis=AX.X, op=ALU.min))
                        mx_done = p.v_mark(sV)
                        p.wait("scalar", sV, mx_done)
                        for qc in range(4):
                            p.S(lambda sc, qc=qc: sc.activation(
                                probs[:, qc * 512:(qc + 1) * 512], pf[qc][:, 0:512],
                                AF.Exp, scale=-1.0, bias=st[:, 4 + qc:5 + qc],
                                accum_out=st[:, 8 + qc:9 + qc]))
                        exp_done = p.s_mark(sS)
                        for qc in range(4):
                            p.bank_release(f"pf{qc}", sS, exp_done)
                        p.wait("vector", sS, exp_done)
                        p.V(lambda v: v.reciprocal(st[:, 8:12], st[:, 8:12]))
                        p.VD()
                        def fnorm(v):
                            for qc in range(4):
                                v.tensor_scalar_mul(probs[:, qc * 512:(qc + 1) * 512],
                                                    probs[:, qc * 512:(qc + 1) * 512],
                                                    st[:, 8 + qc:9 + qc])
                        p.V(fnorm)
                        probs_done = p.v_mark(sV)
                        p.wait("tensor", sV, probs_done)
                        for kcp in range(2):
                            p.bank_acquire("b7")
                            def ftr1(t, kcp=kcp):
                                last = None
                                for k2_ in range(2):
                                    kc = kcp * 2 + k2_
                                    for qc in range(4):
                                        last = t.transpose(
                                            pb16[:, k2_ * 512 + qc * 128:k2_ * 512 + (qc + 1) * 128],
                                            probs[:, qc * 512 + kc * 128:qc * 512 + (kc + 1) * 128],
                                            eye_b[:])
                                return last
                            p.T(ftr1)
                            tmv = p.t_mark(sT)
                            p.wait("vector", sT, tmv)
                            p.V(lambda v, kcp=kcp: v.tensor_copy(
                                probsT[:, kcp * 1024:(kcp + 1) * 1024], pb16[:, 0:1024]))
                            evv = p.v_mark(sV)
                            p.bank_release("b7", sV, evv)
                        p.bank_acquire("b7")
                        def fsv(t, pb=pb, r=r, vblk=vblk):
                            last = None
                            for kc in range(4):
                                last = t.transpose(pb16[:, 512 + kc * 64:512 + (kc + 1) * 64],
                                                   va[pb:pb + 64, vblk * TOK + 512 * kc + r:vblk * TOK + 512 * kc + r + 509:4],
                                                   eye2_b[pb:pb + 64, 0:64])
                            return last
                        p.T(fsv)
                        tmv = p.t_mark(sT)
                        p.wait("vector", sT, tmv)
                        p.V(lambda v: v.tensor_copy(sv[:, 0:256], pb16[:, 512:768]))
                        svv = p.v_mark(sV)
                        p.bank_release("b7", sV, svv)
                        p.wait("tensor", sV, svv)
                        p.bank_acquire("pf6")
                        mm(pf[6][pb:pb + 64, 0:512],
                           [(sv[:, kc * 64:(kc + 1) * 64], probsT[:, kc * 512:(kc + 1) * 512])
                            for kc in range(4)])
                        ctxv = p.t_mark(sT)
                        p.wait("vector", sT, ctxv)
                        p.V(lambda v, pb=pb, r=r, vblk=vblk: v.tensor_add(
                            x_fm[pb:pb + 64, vblk * TOK + r:vblk * TOK + r + 2045:4],
                            x_fm[pb:pb + 64, vblk * TOK + r:vblk * TOK + r + 2045:4],
                            pf[6][pb:pb + 64, 0:512]))
                        last_res = p.v_mark(sV)
                        p.bank_release("pf6", sV, last_res)
                return last_res, sT.c

        def emit_ffn(ci, xln_ready, wr1_free, wsm_free, prev_done):
            g = BIGA
            with nc.psum_tensor(f"pf7_{ci}", [128, 512], F32) as pf7, \
                 nc.sbuf_tensor(f"f_peb{ci}", [128, TOK], F32) as peb, \
                 nc.sbuf_tensor(f"f_st{ci}", [128, 1536], BF16) as stmp, \
                 nc.sbuf_tensor(f"f_y2{ci}", [128, 1536], BF16) as y2tmp:
                with nc.sbuf_tensor(f"f_rw{ci}", [128, 64], F32) as rwf, \
                     nc.sbuf_tensor(f"f_rs{ci}", [128, 32], F32) as rs, \
                     nc.sbuf_tensor(f"f_sr{ci}", [1, 8], F32) as srow, \
                     nc.sbuf_tensor(f"f_sb{ci}", [128, 8], F32) as Sb, \
                     nc.sbuf_tensor(f"f_pr{ci}", [128, 64], F32) as prbB, \
                     nc.sbuf_tensor(f"f_t8{ci}", [128, 64], F32) as t64:
                    p.wait("gpsimd", sV, xln_ready)
                    rw_val = dma_g(rwf[:].rearrange("p (kc c) -> p kc c", kc=8),
                                   bass.AP(fRouter_h, 0, [[E, 128], [128 * E, 8], [1, E]]),
                                   dRW)
                    p.wait("tensor", dRW, rw_val)
                    p.wait("tensor", sV, xln_ready)
                    p.bank_acquire("pf5")
                    mm(pf[5][0:1, 0:8],
                       [(ones_f[:, 0:1], rwf[:, kc * 8:(kc + 1) * 8]) for kc in range(8)])
                    tms = p.t_mark(sT)
                    p.wait("vector", sT, tms)
                    p.V(lambda v: v.tensor_copy(srow[0:1, 0:8], pf[5][0:1, 0:8]))
                    smv = p.v_mark(sV)
                    p.bank_release("pf5", sV, smv)
                    p.wait("tensor", sV, smv)
                    p.bank_acquire("pf5")
                    p.T(lambda t: t.matmul(pf[5][:, 0:8], ones_f[0:1, 0:128],
                                           srow[0:1, 0:8], start=True, stop=True))
                    tms2 = p.t_mark(sT)
                    p.wait("vector", sT, tms2)
                    p.V(lambda v: v.tensor_copy(Sb[:], pf[5][:, 0:8]))
                    sbv = p.v_mark(sV)
                    p.bank_release("pf5", sV, sbv)
                    def pf6_v():
                        return bass.AP(pf[6], 0, [[512, 128], [8, 8], [0 + 1 - 1, 8]]) if False else bass.AP(pf[6], 0, [[512, 128], [8, 8], [1, 8]])
                    def pv3(buf):
                        return bass.AP(buf, 0, [[64, 128], [8, 8], [1, 8]])
                    with nc.sbuf_tensor(f"f_stag{ci}", [64, 128], F32) as stag:
                      for bt in range(2):
                        p.bank_acquire("pf6")
                        for tkl in range(8):
                            tk = bt * 8 + tkl
                            mm(pf[6][0:128, tkl * 8:(tkl + 1) * 8],
                               [(x_fm[:, kc * TOK + tk * 128:kc * TOK + (tk + 1) * 128],
                                 rwf[:, kc * 8:(kc + 1) * 8]) for kc in range(8)])
                        tmv = p.t_mark(sT)
                        p.wait("vector", sT, tmv)
                        st_m = sttbuf[:, bt * 8:(bt + 1) * 8]
                        st_r = sttbuf[:, 16 + bt * 8:16 + (bt + 1) * 8]
                        rbc = bass.AP(sttbuf, 16 + bt * 8, [[32, 128], [1, 8], [0, 8]])
                        mrbc = bass.AP(rs, 0, [[32, 128], [1, 8], [0, 8]])
                        m1bc = bass.AP(rs, 8, [[32, 128], [1, 8], [0, 8]])
                        sumbc = bass.AP(rs, 16, [[32, 128], [1, 8], [0, 8]])
                        m2bc = bass.AP(rs, 24, [[32, 128], [1, 8], [0, 8]])
                        sbbc = bass.AP(Sb, 0, [[8, 128], [0, 8], [1, 8]])
                        p3 = lambda ap: bass.AP(ap.tensor, ap.offset, [[64, 128], [8, 8], [1, 8]])
                        p.V(lambda v, st_m=st_m, st_r=st_r: v.tensor_mul(rs[:, 0:8], st_m, st_r))
                        p.V(lambda v, rbc=rbc: v.tensor_tensor(prbB[:, 0:64], pf6_v(), rbc, op=ALU.mult))
                        p.VD()
                        p.V(lambda v, sbbc=sbbc, mrbc=mrbc: v.tensor_tensor(t64[:, 0:64], sbbc, mrbc, op=ALU.mult))
                        p.VD()
                        p.V(lambda v: v.tensor_add(prbB[:, 0:64], prbB[:, 0:64], t64[:, 0:64]))
                        m1 = p.v_mark(sV)
                        p.bank_release("pf6", sV, m1)
                        p.V(lambda v: v.tensor_reduce(rs[:, 8:16], pv3(prbB), axis=AX.X, op=ALU.max))
                        p.VD()
                        p.V(lambda v, m1bc=m1bc: v.tensor_tensor(prbB[:, 0:64], prbB[:, 0:64], m1bc, op=ALU.subtract))
                        sm1 = p.v_mark(sV)
                        p.wait("scalar", sV, sm1)
                        p.S(lambda sc: sc.activation(prbB[:, 0:64], prbB[:, 0:64], AF.Exp))
                        e1 = p.s_mark(sS)
                        p.wait("vector", sS, e1)
                        p.V(lambda v: v.tensor_reduce(rs[:, 16:24], pv3(prbB), axis=AX.X, op=ALU.add))
                        p.VD()
                        p.V(lambda v: v.reciprocal(rs[:, 16:24], rs[:, 16:24]))
                        p.VD()
                        p.V(lambda v, sumbc=sumbc: v.tensor_tensor(prbB[:, 0:64], prbB[:, 0:64], sumbc, op=ALU.mult))
                        p.VD()
                        p.V(lambda v: v.tensor_reduce(rs[:, 8:16], pv3(prbB), axis=AX.X, op=ALU.max))
                        p.VD()
                        p.V(lambda v, m1bc=m1bc: v.tensor_tensor(t64[:, 0:64], prbB[:, 0:64], m1bc, op=ALU.is_equal))
                        p.VD()
                        p.V(lambda v: v.tensor_scalar_mul(t64[:, 0:64], t64[:, 0:64], 1.0e30))
                        p.VD()
                        p.V(lambda v: v.tensor_sub(t64[:, 0:64], prbB[:, 0:64], t64[:, 0:64]))
                        p.VD()
                        p.V(lambda v: v.tensor_reduce(rs[:, 24:32], pv3(t64), axis=AX.X, op=ALU.max))
                        p.VD()
                        p.V(lambda v, m2bc=m2bc: v.tensor_tensor(t64[:, 0:64], prbB[:, 0:64], m2bc, op=ALU.is_ge))
                        p.VD()
                        p.V(lambda v: v.tensor_mul(prbB[:, 0:64], prbB[:, 0:64], t64[:, 0:64]))
                        lpv = p.v_mark(sV)
                        p.wait("tensor", sV, lpv)
                        p.bank_acquire("pf5")
                        p.T(lambda t: t.transpose(pf[5][0:64, 0:128], prbB[:, 0:64], eye_f[:]))
                        ttv = p.t_mark(sT)
                        p.wait("vector", sT, ttv)
                        p.wait("vector", dPROBS, dPROBS.c)
                        p.V(lambda v: v.tensor_copy(stag[:, 0:128], pf[5][0:64, 0:128]))
                        pfv = p.v_mark(sV)
                        p.bank_release("pf5", sV, pfv)
                        p.wait("sync", sV, pfv)
                        probs_w = dma_y(
                            bass.AP(probs_hbm[ci], bt * 8192, [[128, 64], [1, 128]]),
                            stag[:, 0:128], dPROBS)
                w2_src = w2bf_h if ci == 1 else fW2_h
                w1_src = w1bf_h if ci == 1 else fW1_h
                w_eng = "sync" if ci == 1 else "gpsimd"
                W1S = dW1s2 if ci == 1 else dW1
                W2S = dW2ss2 if ci == 1 else dW2s

                def dma_w(dst, src_ap, sem):
                    return dma_y(dst, src_ap, sem) if ci == 1 else dma_g(dst, src_ap, sem)

                w2_vals = {}
                def load_w2(e, guard):
                    p.wait(w_eng, sT, guard)
                    p.wait(w_eng, dWT2, dWT2.c)
                    v = dma_w(WR1[:, 0:11264].rearrange("p (fc c) -> p fc c", fc=11),
                              bass.AP(w2_src, e * FF * D, [[D, 128], [128 * D, 11], [1, D]]),
                              W2S)
                    w2_vals[e] = v
                    if ci == 0:
                        p.wait("gpsimd", W2S, v)
                        dma_o(bass.AP(w2bf_h, e * FF * D, [[D, 128], [128 * D, 11], [1, D]]),
                              WR1[:, 0:11264].rearrange("p (fc c) -> p fc c", fc=11),
                              dWT2)
                    return v

                w1_vals = {}
                def load_w1(e, fc, slot, guard):
                    p.wait(w_eng, sT, guard)
                    p.wait(w_eng, dWT1[slot], dWT1[slot].c)
                    base = e * D * 2 * FF + fc * 128
                    for half in range(2):
                        dst = bass.AP(WSM, slot * 2048 + half * 128,
                                      [[8192, 128], [256, 8], [1, 128]])
                        src = bass.AP(w1_src, base + half * FF,
                                      [[2 * FF, 128], [128 * 2 * FF, 8], [1, 128]])
                        v = dma_w(dst, src, W1S[slot])
                    w1_vals[(e, fc)] = (slot, v)
                    if ci == 0:
                        p.wait("gpsimd", W1S[slot], v)
                        for half in range(2):
                            dma_o(bass.AP(w1bf_h, base + half * FF,
                                          [[2 * FF, 128], [128 * 2 * FF, 8], [1, 128]]),
                                  bass.AP(WSM, slot * 2048 + half * 128,
                                          [[8192, 128], [256, 8], [1, 128]]),
                                  dWT1[slot])
                    return v

                chunks = [(e, fc) for e in range(E) for fc in range(11)]
                slot_of = {cch: i % 4 for i, cch in enumerate(chunks)}
                load_w2(0, wr1_free)
                for ii in range(4):
                    load_w1(*chunks[ii], slot_of[chunks[ii]], wsm_free)
                pe_vals = {}
                p.wait("sync", dPROBS, probs_w)
                for bt_ in range(2):
                    pe_vals[0] = dma_y(peb[:, bt_ * 1024:(bt_ + 1) * 1024],
                                       bass.AP(probs_hbm[ci], bt_ * 8192 + 0 * 128,
                                               [[0, 128], [1024, 8], [1, 128]]), dPE)

                o_done = prev_done
                tmp_guard = [0, 0, 0]
                for e in range(E):
                    p.wait("vector", dPE, pe_vals[e])
                    p.wait("vector", sT, o_done)
                    def ybank(tc):
                        return pf[tc]
                    def y2bank(tc):
                        return pf[4 + tc] if tc < 3 else pf7
                    def y2key(tc):
                        return f"pf{4 + tc}" if tc < 3 else "b7"
                    for fc in range(11):
                        slot = slot_of[(e, fc)]
                        w1s, w1v = w1_vals[(e, fc)]
                        p.wait("tensor", W1S[w1s], w1v)
                        for tc in range(4):
                            p.bank_acquire(f"pf{tc}")
                            p.bank_acquire(y2key(tc))
                            mm(ybank(tc)[:, 0:512],
                               [(WSM[:, slot * 2048 + kc * 256:slot * 2048 + kc * 256 + 128],
                                 xln[:, kc * TOK + tc * 512:kc * TOK + (tc + 1) * 512])
                                for kc in range(8)])
                            mm(y2bank(tc)[:, 0:512],
                               [(WSM[:, slot * 2048 + kc * 256 + 128:slot * 2048 + kc * 256 + 256],
                                 xln[:, kc * TOK + tc * 512:kc * TOK + (tc + 1) * 512])
                                for kc in range(8)])
                            tmv = p.t_mark(sT)
                            p.wait("scalar", sT, tmv)
                            p.wait("scalar", sV, tmp_guard[tc % 3])
                            p.S(lambda sc, tc=tc: sc.activation(
                                stmp[:, (tc % 3) * 512:(tc % 3 + 1) * 512],
                                ybank(tc)[:, 0:512], AF.Silu))
                            sdone = p.s_mark(sS)
                            p.wait("vector", sT, tmv)
                            p.V(lambda v, tc=tc: v.tensor_mul(
                                y2tmp[:, (tc % 3) * 512:(tc % 3 + 1) * 512],
                                y2bank(tc)[:, 0:512],
                                peb[:, tc * 512:(tc + 1) * 512]))
                            p.VD()
                            p.wait("vector", sS, sdone)
                            p.V(lambda v, fc=fc, tc=tc: v.tensor_mul(
                                g[:, fc * TOK + tc * 512:fc * TOK + (tc + 1) * 512],
                                stmp[:, (tc % 3) * 512:(tc % 3 + 1) * 512],
                                y2tmp[:, (tc % 3) * 512:(tc % 3 + 1) * 512]))
                            vmv = p.v_mark(sV)
                            tmp_guard[tc % 3] = vmv
                            p.bank_release(f"pf{tc}", sV, vmv)
                            p.bank_release(y2key(tc), sV, vmv)
                        chunk_done = sT.c
                        gi = e * 11 + fc
                        if gi + 4 < len(chunks):
                            ne, nfc = chunks[gi + 4]
                            load_w1(ne, nfc, slot_of[(ne, nfc)], chunk_done)
                    g_done = p.v_mark(sV)
                    if e + 1 < E:
                        p.wait("sync", sV, g_done)
                        for bt_ in range(2):
                            pe_vals[e + 1] = dma_y(peb[:, bt_ * 1024:(bt_ + 1) * 1024],
                                                   bass.AP(probs_hbm[ci], bt_ * 8192 + (e + 1) * 128,
                                                           [[0, 128], [1024, 8], [1, 128]]), dPE)
                    p.wait("tensor", sV, g_done)
                    p.wait("tensor", W2S, w2_vals[e])
                    for hc in range(8):
                        for tc in range(4):
                            r4 = (hc * 4 + tc) % 4
                            obank = pf[4 + r4] if r4 < 3 else pf7
                            okey = f"pf{4 + r4}" if r4 < 3 else "b7"
                            p.bank_acquire(okey)
                            def fo(t, hc=hc, tc=tc, obank=obank):
                                last = None
                                for fc in range(11):
                                    last = t.matmul(obank[:, 0:512],
                                                    WR1[:, fc * 1024 + hc * 128:fc * 1024 + (hc + 1) * 128],
                                                    g[:, fc * TOK + tc * 512:fc * TOK + (tc + 1) * 512],
                                                    start=(fc == 0), stop=(fc == 10))
                                return last
                            p.T(fo)
                            tmv = p.t_mark(sT)
                            p.wait("vector", sT, tmv)
                            p.V(lambda v, hc=hc, tc=tc, obank=obank: v.scalar_tensor_tensor(
                                x_fm[:, hc * TOK + tc * 512:hc * TOK + (tc + 1) * 512],
                                obank[:, 0:512], 1.0,
                                x_fm[:, hc * TOK + tc * 512:hc * TOK + (tc + 1) * 512],
                                op0=ALU.mult, op1=ALU.add))
                            vmv = p.v_mark(sV)
                            p.bank_release(okey, sV, vmv)
                    o_done = sT.c
                    if e + 1 < E:
                        load_w2(e + 1, o_done)
                res = p.v_mark(sV)
                return res, o_done

        def emit_attn_b(xln_ready, prev_tdone, wr1_free, wsm_free):
            vh = BIGA[:, 0:8704]
            ctx = BIGA[:, 8704:16896]
            Mi = BIGA[:, 16896:19136]
            Me0 = BIGA[:, 19136:21376]
            Me15 = BIGA[:, 21376:23616]
            xlh = BIGA[:, 23616:23712]
            with nc.psum_tensor("pb16b", [128, 1024], BF16) as pb16, \
                 nc.sbuf_tensor("ab_pads", [128, 2048], BF16) as pads, \
                 nc.sbuf_tensor("ab_q2", [128, TOK], BF16) as q2, \
                 nc.sbuf_tensor("ab_k2", [128, 2060], BF16) as k2, \
                 nc.sbuf_tensor("ab_cc", [128, 96], F32) as ccs, \
                 nc.sbuf_tensor("ab_s2", [128, 280], F32) as s2b, \
                 nc.sbuf_tensor("ab_ph", [128, 8 * 140], BF16) as ph, \
                 nc.sbuf_tensor("ab_pt", [128, 512], BF16) as pT, \
                 nc.sbuf_tensor("ab_dn", [128, 24], F32) as dn:
                pad0 = pads[:, 0:1024]
                pad16 = pads[:, 1024:2048]
                p.bank_acquire("b7")
                p.V(lambda v: v.memzero(pb16[:, 0:1024]))
                zv = p.v_mark(sV)
                p.bank_release("b7", sV, zv)
                p.wait("gpsimd", sT, prev_tdone)
                for h in range(H):
                    dma_g(Mi[:, h * 140:(h + 1) * 140], Mi_h.ap()[h], dM)
                    dma_g(Me0[:, h * 140:(h + 1) * 140], Me0_h.ap()[h], dM)
                    dma_g(Me15[:, h * 140:(h + 1) * 140], Me15_h.ap()[h], dM)
                m_done = dM.c

                p.wait("vector", sV, xln_ready)
                p.wait("vector", dSEL, sel_done)
                def fcc(v):
                    for kc in range(8):
                        v.tensor_single_scalar(ccs[:, kc * 12:kc * 12 + 6],
                                               xln[:, kc * TOK + 2042:kc * TOK + 2048],
                                               selbuf[:, 0:1], op=ALU.mult)
                        v.tensor_single_scalar(ccs[:, kc * 12 + 6:kc * 12 + 12],
                                               xln[:, kc * TOK + 0:kc * TOK + 6],
                                               selbuf[:, 1:2], op=ALU.mult)
                p.V(fcc)
                ccv = p.v_mark(sV)
                p.wait("gpsimd", sV, ccv)
                cc1 = dma_o(cc_in.ap(), ccs[:])
                p.wait("gpsimd", dO, cc1)
                ccg = sG.bump()
                p.G(lambda g_, s=sG.h: g_.collective_compute(
                    "AllReduce", ALU.add,
                    replica_groups=[[0, 1], [2, 3], [4, 5], [6, 7]],
                    ins=[cc_in.ap().opt()],
                    outs=[cc_out.ap().opt()]).then_inc(s))
                p.wait("gpsimd", sG, ccg)
                cc2 = dma_o(ccs[:], cc_out.ap())
                p.wait("vector", dO, cc2)
                p.wait("vector", sT, prev_tdone)
                def fxh(v):
                    for kc in range(8):
                        v.tensor_single_scalar(xlh[:, kc * 12:kc * 12 + 6],
                                               ccs[:, kc * 12:kc * 12 + 6],
                                               selbuf[:, 1:2], op=ALU.mult)
                        v.tensor_single_scalar(xlh[:, kc * 12 + 6:kc * 12 + 12],
                                               ccs[:, kc * 12 + 6:kc * 12 + 12],
                                               selbuf[:, 0:1], op=ALU.mult)
                p.V(fxh)
                p.VD()
                def fpads(v):
                    for kc in range(8):
                        v.tensor_copy(pad0[:, kc * 128:kc * 128 + 6], xlh[:, kc * 12:kc * 12 + 6])
                        v.tensor_copy(pad0[:, kc * 128 + 6:kc * 128 + 128],
                                      xln[:, kc * TOK:kc * TOK + 122])
                        v.tensor_copy(pad16[:, kc * 128:kc * 128 + 6],
                                      xln[:, kc * TOK + 2042:kc * TOK + 2048])
                        v.tensor_copy(pad16[:, kc * 128 + 6:kc * 128 + 12],
                                      xlh[:, kc * 12 + 6:kc * 12 + 12])
                        v.memset(pad16[:, kc * 128 + 12:kc * 128 + 128], 0.0)
                p.V(fpads)
                halo_done = p.v_mark(sV)

                qk_rel = [wsm_free, wsm_free]
                wv_guard = wr1_free
                last_res = 0
                for hf in range(2):
                    p.wait("gpsimd", sT, wv_guard)
                    p.wait("gpsimd", dWT2, dWT2.c)
                    wv_val = dma_g(WR1[:, 0:4096].rearrange("p (kc c) -> p kc c", kc=8),
                                   bass.AP(abWqkv_h, 2048 + hf * 512,
                                           [[3 * D, 128], [128 * 3 * D, 8], [1, 512]]),
                                   dWV)
                    p.wait("tensor", dWV, wv_val)
                    p.wait("tensor", dM, m_done)
                    for t16 in list(range(1, 16)) + [0, 16]:
                        ib = t16 % 2
                        p.bank_acquire(f"pf{ib}")
                        if t16 == 0:
                            p.wait("tensor", sV, halo_done)
                            lhs = lambda kc: pad0[:, kc * 128:(kc + 1) * 128]
                        elif t16 == 16:
                            lhs = lambda kc: pad16[:, kc * 128:(kc + 1) * 128]
                        else:
                            lhs = lambda kc, t16=t16: xln[:, kc * TOK + t16 * 128 - 6:kc * TOK + (t16 + 1) * 128 - 6]
                        mm(pf[ib][:, 0:512],
                           [(lhs(kc), WR1[:, kc * 512:(kc + 1) * 512]) for kc in range(8)],
                           extra=(ones_b[0:1, 0:128], abBv_row[0:1, hf * 512:(hf + 1) * 512]))
                        tmv = p.t_mark(sT)
                        p.wait("vector", sT, tmv)
                        p.V(lambda v, t16=t16, ib=ib: v.tensor_copy(
                            vh[:, t16 * 512:(t16 + 1) * 512], pf[ib][:, 0:512]))
                        vmv = p.v_mark(sV)
                        p.bank_release(f"pf{ib}", sV, vmv)
                    vh_done = p.v_mark(sV)
                    wv_guard = sT.c

                    ctx_marks = []
                    for prl in range(4):
                        pr = hf * 4 + prl
                        slot = prl % 2
                        p.wait("gpsimd", sT, qk_rel[slot])
                        p.wait("gpsimd", dWT1[slot], dWT1[slot].c)
                        base = slot * 2048
                        dma_g(WSM[:, base:base + 1024].rearrange("p (kc c) -> p kc c", kc=8),
                              bass.AP(abWqkv_h, pr * 128, [[3 * D, 128], [128 * 3 * D, 8], [1, 128]]),
                              dQK[slot])
                        qk_val = dma_g(WSM[:, base + 1024:base + 2048].rearrange("p (kc c) -> p kc c", kc=8),
                                       bass.AP(abWqkv_h, D + pr * 128, [[3 * D, 128], [128 * 3 * D, 8], [1, 128]]),
                                       dQK[slot])
                        p.wait("tensor", dQK[slot], qk_val)
                        for which in range(2):
                            for tc in range(4):
                                ib = 2 + tc % 2
                                p.bank_acquire(f"pf{ib}")
                                mm(pf[ib][:, 0:512],
                                   [(WSM[:, base + which * 1024 + kc * 128:base + which * 1024 + (kc + 1) * 128],
                                     xln[:, kc * TOK + tc * 512:kc * TOK + (tc + 1) * 512])
                                    for kc in range(8)])
                                tmv = p.t_mark(sT)
                                p.wait("vector", sT, tmv)
                                if which == 0:
                                    p.V(lambda v, tc=tc, ib=ib, pr=pr: v.tensor_scalar(
                                        q2[:, tc * 512:(tc + 1) * 512], pf[ib][:, 0:512],
                                        abB[:, pr:pr + 1], -SCALE, op0=ALU.add, op1=ALU.mult))
                                else:
                                    p.V(lambda v, tc=tc, ib=ib, pr=pr: v.tensor_scalar_add(
                                        k2[:, 6 + tc * 512:6 + (tc + 1) * 512], pf[ib][:, 0:512],
                                        abB[:, 8 + pr:9 + pr]))
                                vmv = p.v_mark(sV)
                                p.bank_release(f"pf{ib}", sV, vmv)
                        p.bank_acquire("pf4")
                        mm(pf[4][:, 0:12],
                           [(WSM[:, base + 1024 + kc * 128:base + 1024 + (kc + 1) * 128],
                             xlh[:, kc * 12:(kc + 1) * 12]) for kc in range(8)])
                        tmv = p.t_mark(sT)
                        p.wait("vector", sT, tmv)
                        p.V(lambda v, pr=pr: v.tensor_scalar_add(
                            k2[:, 0:6], pf[4][:, 0:6], abB[:, 8 + pr:9 + pr]))
                        p.V(lambda v, pr=pr: v.tensor_scalar_add(
                            k2[:, 2054:2060], pf[4][:, 6:12], abB[:, 8 + pr:9 + pr]))
                        vmv = p.v_mark(sV)
                        p.bank_release("pf4", sV, vmv)
                        qk_rel[slot] = sT.c
                        qkv_ready = p.v_mark(sV)
                        p.wait("tensor", sV, qkv_ready)
                        p.wait("tensor", sV, vh_done)

                        for hh in range(2):
                            h = 2 * pr + hh
                            pb = (h % 2) * 64
                            lh = h - hf * 8
                            for cb in range(2):
                                exp_marks = []
                                batch_tstart = None
                                for c8 in range(8):
                                    ch = cb * 8 + c8
                                    ib = ch % 4
                                    Msel = Me0 if ch == 0 else (Me15 if ch == 15 else Mi)
                                    p.bank_acquire(f"pf{ib}")
                                    def fsc(t, ib=ib, pb=pb, ch=ch, h=h, Msel=Msel):
                                        t.matmul(pf[ib][:, 0:140],
                                                 q2[pb:pb + 64, ch * 128:(ch + 1) * 128],
                                                 k2[pb:pb + 64, ch * 128:ch * 128 + 140],
                                                 start=True, stop=False)
                                        return t.matmul(pf[ib][:, 0:140], eyen_b[:],
                                                        Msel[:, h * 140:(h + 1) * 140],
                                                        start=False, stop=True)
                                    p.T(fsc)
                                    tmv = p.t_mark(sT)
                                    p.wait("vector", sT, tmv)
                                    p.V(lambda v, c8=c8, ib=ib: v.tensor_reduce(
                                        dn[:, c8:c8 + 1], pf[ib][:, 0:140], axis=AX.X, op=ALU.min))
                                    s2v = p.v_mark(sV)
                                    p.wait("scalar", sV, s2v)
                                    p.S(lambda sc, c8=c8, ib=ib: sc.activation(
                                        ph[:, c8 * 140:(c8 + 1) * 140], pf[ib][:, 0:140],
                                        AF.Exp, scale=-1.0, bias=dn[:, c8:c8 + 1],
                                        accum_out=dn[:, 8 + c8:9 + c8]))
                                    em = p.s_mark(sS)
                                    exp_marks.append(em)
                                    p.bank_release(f"pf{ib}", sS, em)
                                p.wait("vector", sS, exp_marks[-1])
                                p.V(lambda v: v.reciprocal(dn[:, 16:24], dn[:, 8:16]))
                                p.VD()
                                def fnm(v):
                                    for c8 in range(8):
                                        v.tensor_scalar_mul(ph[:, c8 * 140:(c8 + 1) * 140],
                                                            ph[:, c8 * 140:(c8 + 1) * 140],
                                                            dn[:, 16 + c8:17 + c8])
                                p.V(fnm)
                                nv = p.v_mark(sV)
                                p.wait("tensor", sV, nv)
                                p.bank_acquire("pf5")
                                for c8 in range(8):
                                    ch = cb * 8 + c8
                                    pts = (c8 % 2) * 256
                                    if c8 % 2 == 0:
                                        p.bank_acquire("b7")
                                        def ftr(t, c8=c8):
                                            for j in range(2):
                                                t.transpose(pb16[:, j * 256:j * 256 + 128],
                                                            ph[:, (c8 + j) * 140:(c8 + j) * 140 + 128], eye_b[:])
                                                last = t.transpose(pb16[0:12, j * 256 + 128:j * 256 + 256],
                                                                   ph[:, (c8 + j) * 140 + 128:(c8 + j) * 140 + 140], eye_b[:])
                                            return last
                                        p.T(ftr)
                                        tmv = p.t_mark(sT)
                                        p.wait("vector", sT, tmv)
                                        p.V(lambda v: v.tensor_copy(pT[:, 0:512], pb16[:, 0:512]))
                                        ptv = p.v_mark(sV)
                                        p.bank_release("b7", sV, ptv)
                                        p.wait("tensor", sV, ptv)
                                    def fctx(t, ch=ch, lh=lh, pb=pb, c8=c8, pts=pts):
                                        t.matmul(pf[5][pb:pb + 64, (c8 % 4) * 128:(c8 % 4 + 1) * 128],
                                                 vh[:, ch * 512 + lh * 64:ch * 512 + (lh + 1) * 64],
                                                 pT[:, pts:pts + 128], start=True, stop=False)
                                        return t.matmul(pf[5][pb:pb + 64, (c8 % 4) * 128:(c8 % 4 + 1) * 128],
                                                        vh[0:12, (ch + 1) * 512 + lh * 64:(ch + 1) * 512 + (lh + 1) * 64],
                                                        pT[0:12, pts + 128:pts + 256], start=False, stop=True)
                                    p.T(fctx)
                                    tctx = p.t_mark(sT)
                                    if c8 % 4 == 3:
                                        p.wait("vector", sT, tctx)
                                        ch0 = cb * 8 + c8 - 3
                                        p.V(lambda v, pb=pb, lh=lh, hf=hf, ch0=ch0: v.tensor_copy(
                                            ctx[pb:pb + 64, (lh // 2) * TOK + ch0 * 128:(lh // 2) * TOK + (ch0 + 4) * 128],
                                            pf[5][pb:pb + 64, 0:512]))
                                        cvv = p.v_mark(sV)
                                        p.bank_release("pf5", sV, cvv)
                                        if c8 < 7:
                                            p.bank_acquire("pf5")
                                ctx_marks.append(sV.c)
                    ctx_done = p.v_mark(sV)
                    pj_guard = sT.c
                    p.wait("gpsimd", sT, pj_guard)
                    pj_val = dma_g(WR1[:, 0:4096].rearrange("p (kc c) -> p kc c", kc=8),
                                   bass.AP(abProjW_h, hf * 512 * D,
                                           [[D, 128], [128 * D, 4], [1, D]]),
                                   dPJ)
                    p.wait("tensor", dPJ, pj_val)
                    p.wait("tensor", sV, ctx_done)
                    for hc in range(8):
                        for tc in range(4):
                            ib = 2 + (hc * 4 + tc) % 2
                            p.bank_acquire(f"pf{ib}")
                            mm(pf[ib][:, 0:512],
                               [(WR1[:, kc4 * 1024 + hc * 128:kc4 * 1024 + (hc + 1) * 128],
                                 ctx[:, kc4 * TOK + tc * 512:kc4 * TOK + (tc + 1) * 512])
                                for kc4 in range(4)])
                            tmv = p.t_mark(sT)
                            p.wait("vector", sT, tmv)
                            if hf == 0:
                                p.V(lambda v, hc=hc, tc=tc, ib=ib: v.scalar_tensor_tensor(
                                    x_fm[:, hc * TOK + tc * 512:hc * TOK + (tc + 1) * 512],
                                    pf[ib][:, 0:512], 1.0,
                                    x_fm[:, hc * TOK + tc * 512:hc * TOK + (tc + 1) * 512],
                                    op0=ALU.mult, op1=ALU.add))
                            else:
                                p.V(lambda v, hc=hc, tc=tc, ib=ib: v.scalar_tensor_tensor(
                                    x_fm[:, hc * TOK + tc * 512:hc * TOK + (tc + 1) * 512],
                                    pf[ib][:, 0:512], abProjB[:, hc:hc + 1],
                                    x_fm[:, hc * TOK + tc * 512:hc * TOK + (tc + 1) * 512],
                                    op0=ALU.add, op1=ALU.add))
                            last_res = p.v_mark(sV)
                            p.bank_release(f"pf{ib}", sV, last_res)
                    wv_guard = sT.c
                return last_res, sT.c

        dep = [(sV, x_ready)]
        tdone = 0
        wr1_free = 0
        wsm_free = 0
        if n_sub >= 1:
            lr = emit_ln(dep)
            m, tdone = emit_attn_a(lr, wr1_free)
            dep = [(sV, m)]
            if 0 in taps:
                p.wait("sync", sV, m)
                dma_y(dbg_h[0].ap(), x_fm[:])
        if n_sub >= 2:
            lr = emit_ln(dep, want_stt=True)
            m, tdone = emit_ffn(0, lr, tdone, tdone, tdone)
            dep = [(sV, m)]
            if 1 in taps:
                p.wait("sync", sV, m)
                dma_y(dbg_h[1].ap(), x_fm[:])
        if n_sub >= 3:
            lr = emit_ln(dep)
            m, tdone = emit_attn_b(lr, tdone, tdone, tdone)
            dep = [(sV, m)]
            if 2 in taps:
                p.wait("sync", sV, m)
                dma_y(dbg_h[2].ap(), x_fm[:])
        if n_sub >= 4:
            lr = emit_ln(dep, want_stt=True)
            m, tdone = emit_ffn(1, lr, tdone, tdone, tdone)
            dep = [(sV, m)]
            if 3 in taps:
                p.wait("sync", sV, m)
                dma_y(dbg_h[3].ap(), x_fm[:])

        with nc.sbuf_tensor("ostage", [128, 4 * D], F32) as ostage:
            for sem, val in dep:
                p.wait("tensor", sem, val)
            for rd in range(4):
                if rd > 0:
                    p.wait("vector", dO, dO.c)
                for i in range(4):
                    ti = rd * 4 + i
                    for kcg in range(2):
                        ib = (i * 2 + kcg) % 4
                        p.bank_acquire(f"pf{ib}")
                        def fot(t, ti=ti, kcg=kcg, ib=ib):
                            last = None
                            for j in range(4):
                                kc = kcg * 4 + j
                                last = t.transpose(pf[ib][:, j * 128:(j + 1) * 128],
                                                   x_fm[:, kc * TOK + ti * 128:kc * TOK + (ti + 1) * 128],
                                                   eye_f[:])
                            return last
                        p.T(fot)
                        tmv = p.t_mark(sT)
                        p.wait("vector", sT, tmv)
                        p.V(lambda v, i=i, kcg=kcg, ib=ib: v.tensor_copy(
                            ostage[:, i * D + kcg * 512:i * D + (kcg + 1) * 512],
                            pf[ib][:, 0:512]))
                        vmv = p.v_mark(sV)
                        p.bank_release(f"pf{ib}", sV, vmv)
                ov = p.v_mark(sV)
                p.wait("gpsimd", sV, ov)
                for i in range(4):
                    ti = rd * 4 + i
                    dma_o(out_h.ap()[ti * 128:(ti + 1) * 128, :],
                          ostage[:, i * D:(i + 1) * D])
        p.wait("gpsimd", dO, dO.c)

        with nc.Block() as block:
            @block.tensor
            def _(t):
                for fn in p.q["tensor"]:
                    fn(t)

            @block.vector
            def _(v):
                for fn in p.q["vector"]:
                    fn(v)

            @block.scalar
            def _(s):
                for fn in p.q["scalar"]:
                    fn(s)

            @block.gpsimd
            def _(g):
                for fn in p.q["gpsimd"]:
                    fn(g)

            @block.sync
            def _(s):
                for fn in p.q["sync"]:
                    fn(s)

    return nc


def _build_M(rpb, seg):
    l = np.arange(128)[:, None]
    c = np.arange(140)[None, :]
    bidx = np.clip(6 + c - l, 0, 24)
    gath = np.take(rpb, bidx.reshape(-1), axis=1).reshape(rpb.shape[0], 128, 140)
    valid_i = (c - l >= 0) & (c - l <= 12)
    Mi = np.where(valid_i[None], gath, np.float32(NEG))
    Me0 = Mi.copy()
    Me15 = Mi.copy()
    if seg == 0:
        v0 = (c >= 6) & (c <= 18)
        patch = np.where(v0[None], gath, np.float32(NEG))
        Me0[:, 0:6, :] = patch[:, 0:6, :]
    else:
        v15 = (c >= 121) & (c <= 133)
        patch = np.where(v15[None], gath, np.float32(NEG))
        Me15[:, 122:128, :] = patch[:, 122:128, :]
    return (np.ascontiguousarray(Mi.astype(np.float32)),
            np.ascontiguousarray(Me0.astype(np.float32)),
            np.ascontiguousarray(Me15.astype(np.float32)))


def kernel(L0_attn_a_X, L0_attn_a_Wq, L0_attn_a_bq, L0_attn_a_Wkv, L0_attn_a_bkv,
           L0_attn_b_qkv_w, L0_attn_b_qkv_b, L0_attn_b_rpb, L0_attn_b_proj_w,
           L0_attn_b_proj_b, L0_ffn_router_weight, L0_ffn_w1, L0_ffn_w2):
    key = (N_SUB, tuple(DEBUG_TAPS))
    if key not in _CACHE:
        _CACHE[key] = build_program(N_SUB, tuple(DEBUG_TAPS))
    nc = _CACHE[key]

    X = np.asarray(L0_attn_a_X, dtype=np.float32)
    eye = np.eye(128, dtype=np.float32)
    eye2 = np.ascontiguousarray(np.concatenate([np.eye(64, dtype=np.float32)] * 2, axis=0))
    shared = dict(
        aaWq=np.ascontiguousarray(np.asarray(L0_attn_a_Wq, np.float32)),
        aaWkv=np.ascontiguousarray(np.asarray(L0_attn_a_Wkv, np.float32)),
        aaBq=np.ascontiguousarray(np.asarray(L0_attn_a_bq, np.float32).reshape(8, 128).T),
        aaBkv=np.ascontiguousarray(np.asarray(L0_attn_a_bkv, np.float32).reshape(16, 128).T),
        abWqkv=np.ascontiguousarray(np.asarray(L0_attn_b_qkv_w, np.float32).T),
        abB=np.ascontiguousarray(np.asarray(L0_attn_b_qkv_b, np.float32).reshape(24, 128).T),
        abBv_row=np.ascontiguousarray(np.asarray(L0_attn_b_qkv_b, np.float32)[2048:3072].reshape(1, 1024)),
        abProjW=np.ascontiguousarray(np.asarray(L0_attn_b_proj_w, np.float32).T),
        abProjB=np.ascontiguousarray(np.asarray(L0_attn_b_proj_b, np.float32).reshape(8, 128).T),
        fRouter=np.ascontiguousarray(np.asarray(L0_ffn_router_weight, np.float32)),
        fW1=np.ascontiguousarray(np.asarray(L0_ffn_w1, np.float32)),
        fW2=np.ascontiguousarray(np.asarray(L0_ffn_w2, np.float32)),
        eye=eye, eye2=eye2,
    )
    rpb = np.asarray(L0_attn_b_rpb, np.float32)
    Ms = {s: _build_M(rpb, s) for s in (0, 1)}
    in_maps = []
    for c in range(8):
        b, seg = c // 2, c % 2
        m = dict(shared)
        m["x0"] = np.ascontiguousarray(X[b, seg * TOK:(seg + 1) * TOK])
        m["Mi"], m["Me0"], m["Me15"] = Ms[seg]
        m["sel"] = np.asarray([[1.0, 0.0]] if seg == 0 else [[0.0, 1.0]], np.float32)
        in_maps.append(m)

    res = run_bass_kernel_spmd(nc, in_maps, core_ids=list(range(8)))
    out = np.empty((4, 4096, 1024), np.float32)
    for c in range(8):
        b, seg = c // 2, c % 2
        out[b, seg * TOK:(seg + 1) * TOK] = res.results[c]["out"]
    if DEBUG_TAPS:
        kernel.last_dbg = [{i: r[f"dbg{i}"] for i in DEBUG_TAPS} for r in res.results]
    return out
